# version 05 (final)
# baseline (speedup 1.0000x reference)
import math
import contextlib
import numpy as np
import ml_dtypes
import concourse.bass as bass
import concourse.mybir as mybir
from concourse.bass_utils import run_bass_kernel_spmd

F32 = mybir.dt.float32
BF16 = mybir.dt.bfloat16
AF = mybir.ActivationFunctionType
ALU = mybir.AluOpType
AX = mybir.AxisListType

N_CORES = 8
D = 1024
T_FULL = 16384
B = 2
T_CORE = T_FULL // N_CORES

BF16_NP = ml_dtypes.bfloat16


class Prog:
    ENGINES = ("tensor", "vector", "scalar", "gpsimd", "sync")

    def __init__(self, nc):
        self.nc = nc
        self.items = []
        self.cnt = 0
        self.stack = contextlib.ExitStack()
        self._uid = 0

    def _nm(self, name):
        self._uid += 1
        return f"{name}__{self._uid}"

    def sb(self, name, shape, dtype):
        return self.stack.enter_context(
            self.nc.sbuf_tensor(self._nm(name), shape, dtype))

    def cvec(self, val, dtype=F32):
        key = (float(val), dtype)
        if not hasattr(self, "_consts"):
            self._consts = {}
        if key not in self._consts:
            t = self.sb(f"cv{len(self._consts)}", [128, 1], dtype)
            self.memset(t[:], float(val), engine="gpsimd")
            self._consts[key] = t
        return self._consts[key][:, 0:1]

    def ps(self, name, shape, dtype=F32):
        return self.stack.enter_context(
            self.nc.psum_tensor(self._nm(name), shape, dtype))

    def scope(self):
        return contextlib.ExitStack()

    def sbs(self, scope, name, shape, dtype):
        return scope.enter_context(
            self.nc.sbuf_tensor(self._nm(name), shape, dtype))

    def pss(self, scope, name, shape, dtype=F32):
        return scope.enter_context(
            self.nc.psum_tensor(self._nm(name), shape, dtype))

    def switch_dmas(self, fn, ndmas, engine="gpsimd"):
        self.items.append((engine, ("switch", fn, ndmas), self.cnt,
                           16 * ndmas))
        self.cnt += 16 * ndmas

    def collective(self, kind, ins, outs, op=ALU.bypass):
        def closure(e):
            return e.collective_compute(
                kind, op, replica_groups=[list(range(N_CORES))],
                ins=ins, outs=outs)
        self.emit("gpsimd", closure, 16)

    def emit(self, engine, closure, inc=1):
        self.items.append((engine, closure, self.cnt, inc))
        self.cnt += inc

    def dma(self, out, in_, engine="sync", **kw):
        self.emit(engine, lambda e: e.dma_start(out=out, in_=in_, **kw), 16)

    def mm(self, out, lhsT, rhs, start, stop, **kw):
        self.emit("tensor",
                  lambda e: e.matmul(out, lhsT, rhs, start=start, stop=stop,
                                     **kw), 1)

    def transpose(self, out, in_, ident, **kw):
        self.emit("tensor", lambda e: e.transpose(out, in_, ident, **kw), 1)

    def act(self, out, in_, func, bias=0.0, scale=1.0, **kw):
        self.emit("scalar",
                  lambda e: e.activation(out, in_, func, bias=bias,
                                         scale=scale, **kw), 1)

    def tt(self, out, in0, in1, op, engine="vector"):
        self.emit(engine, lambda e: e.tensor_tensor(out, in0, in1, op), 1)

    def ts(self, out, in0, s1, s2, op0, op1=ALU.bypass, engine="vector"):
        self.emit(engine,
                  lambda e: e.tensor_scalar(out, in0, s1, s2, op0, op1), 1)

    def stt(self, out, in0, scalar, in1, op0, op1, engine="vector"):
        self.emit(engine,
                  lambda e: e.scalar_tensor_tensor(out, in0, scalar, in1,
                                                   op0, op1), 1)

    def copy(self, out, in_, engine="vector"):
        self.emit(engine, lambda e: e.tensor_copy(out, in_), 1)

    def memset(self, ap, const, engine="vector"):
        self.emit(engine, lambda e: e.memset(ap, const), 1)

    def recip(self, out, in_):
        self.emit("vector", lambda e: e.reciprocal(out, in_), 1)

    def addcol(self, out, in_, col):
        self.emit("scalar", lambda e: e.add(out, in_, col), 1)

    def finish(self):
        nc = self.nc
        with nc.semaphore("gchain") as G, nc.Block() as block:
            per_engine = {e: [] for e in self.ENGINES}
            for it in self.items:
                per_engine[it[0]].append(it)

            run_last = {}
            if True:
                idx_of = {}
                gi = 0
                for it in self.items:
                    idx_of[id(it)] = gi
                    gi += 1
                i = 0
                n = len(self.items)
                while i < n:
                    eng, closure, before, inc = self.items[i]
                    if eng == "tensor" and not isinstance(closure, tuple):
                        j = i
                        tot = 0
                        while j < n and self.items[j][0] == "tensor" and \
                                not isinstance(self.items[j][1], tuple):
                            tot += self.items[j][3]
                            j += 1
                        for k in range(i, j):
                            run_last[id(self.items[k])] = \
                                (k == j - 1, tot if k == j - 1 else 0,
                                 k == i)
                        i = j
                    else:
                        i += 1

            def make_section(eng_name):
                def section(eng):
                    for item in per_engine[eng_name]:
                        closure, before, inc = item[1], item[2], item[3]
                        rl = run_last.get(id(item))
                        if rl is not None:
                            is_last, tot, is_first = rl
                            if is_first and before > 0:
                                eng.wait_ge(G, before)
                            inst = closure(eng)
                            if is_last:
                                inst.then_inc(G, tot)
                            continue
                        if before > 0:
                            eng.wait_ge(G, before)
                        if isinstance(closure, tuple) and \
                                closure[0] == "switch":
                            _, fn, ndmas = closure
                            for arm in nc.switch_core_id([eng], n=128):
                                role = arm.logical % N_CORES
                                pairs = fn(role)
                                assert len(pairs) == ndmas
                                for o_ap, i_ap in pairs:
                                    eng.dma_start(out=o_ap, in_=i_ap
                                                  ).then_inc(G, 16)
                            continue
                        inst = closure(eng)
                        inst.then_inc(G, inc)
                    eng.wait_ge(G, self.cnt)
                return section

            for name in self.ENGINES:
                if per_engine[name]:
                    getattr(block, name)(make_section(name))
        self.stack.close()
        return nc


def decl_in(nc, name, shape):
    return nc.dram_tensor(name, list(shape), F32, kind="ExternalInput").ap()


def decl_in_bf(nc, name, shape):
    return nc.dram_tensor(name, list(shape), BF16, kind="ExternalInput").ap()


def decl_out(nc, name, shape):
    return nc.dram_tensor(name, list(shape), F32, kind="ExternalOutput").ap()


def load_bias_col(p, hbm_ap, n, name, sc=None):
    t = p.sb(name, [128, n // 128], F32) if sc is None else \
        p.sbs(sc, name, [128, n // 128], F32)
    p.dma(t[:], hbm_ap.rearrange("(c p) -> p c", p=128),
          allow_slow_non_contiguous=True)
    return t


def load_w_bf(p, w_ap, K, M, name, stage_cols=1536, sc=None):
    kb = (K + 127) // 128
    if sc is None:
        tiles = [p.sb(f"{name}_k{k}", [128, M], BF16) for k in range(kb)]
    else:
        tiles = [p.sbs(sc, f"{name}_k{k}", [128, M], BF16)
                 for k in range(kb)]
    sc = p.scope()
    stage = p.sbs(sc, f"{name}_stage", [128, min(M, stage_cols)], F32)
    for k in range(kb):
        ksz = min(128, K - k * 128)
        c0 = 0
        while c0 < M:
            cw = min(M - c0, stage.shape[1])
            p.dma(stage[0:ksz, 0:cw], w_ap[k * 128:k * 128 + ksz,
                                           c0:c0 + cw])
            p.copy(tiles[k][0:ksz, c0:c0 + cw], stage[0:ksz, 0:cw],
                   engine="gpsimd")
            c0 += cw
    sc.close()
    return tiles


class LNScratch:
    def __init__(self, p, W, tag=""):
        self.mean = p.sb(f"ln_mean{tag}", [128, W], F32)
        self.ex2 = p.sb(f"ln_ex2{tag}", [128, W], F32)
        self.rstd = p.sb(f"ln_rstd{tag}", [128, W], F32)
        self.tmp = p.sb(f"ln_tmp{tag}", [128, W], F32)
        self.xb = [p.sb(f"ln_xb{i}{tag}", [128, W], BF16) for i in range(8)]
        self.sqt = p.sb(f"ln_sq{tag}", [128, 512], BF16)
        self.ps_s = p.ps(f"ln_pss{tag}", [128, 512], F32)
        self.ps_q = p.ps(f"ln_psq{tag}", [128, 512], F32)


def emit_ln(p, x_tiles, W, sc, ones_bf, out_bf=None, out_f32=None,
            affine=None, x_bf16=False):
    if x_bf16:
        xb = x_tiles
    else:
        xb = sc.xb
        for blk in range(8):
            p.copy(xb[blk][:, 0:W], x_tiles[blk][:, 0:W], engine="gpsimd")
    nch = (W + 511) // 512
    for c in range(nch):
        c0 = c * 512
        cw = min(512, W - c0)
        for blk in range(8):
            p.mm(sc.ps_s[:, 0:cw], ones_bf[:, :], xb[blk][:, c0:c0 + cw],
                 start=(blk == 0), stop=(blk == 7))
        for blk in range(8):
            p.act(sc.sqt[:, 0:cw], xb[blk][:, c0:c0 + cw], AF.Square)
            p.mm(sc.ps_q[:, 0:cw], ones_bf[:, :], sc.sqt[:, 0:cw],
                 start=(blk == 0), stop=(blk == 7))
        p.act(sc.mean[:, c0:c0 + cw], sc.ps_s[:, 0:cw], AF.Copy,
              scale=1.0 / 1024.0)
        p.act(sc.ex2[:, c0:c0 + cw], sc.ps_q[:, 0:cw], AF.Copy,
              scale=1.0 / 1024.0)
    p.tt(sc.tmp[:, 0:W], sc.mean[:, 0:W], sc.mean[:, 0:W], ALU.mult)
    p.tt(sc.ex2[:, 0:W], sc.ex2[:, 0:W], sc.tmp[:, 0:W], ALU.subtract)
    p.act(sc.rstd[:, 0:W], sc.ex2[:, 0:W], AF.Sqrt, bias=p.cvec(1e-5))
    p.recip(sc.rstd[:, 0:W], sc.rstd[:, 0:W])
    for blk in range(8):
        p.tt(sc.tmp[:, 0:W], x_tiles[blk][:, 0:W], sc.mean[:, 0:W],
             ALU.subtract)
        if affine is not None:
            p.tt(sc.tmp[:, 0:W], sc.tmp[:, 0:W], sc.rstd[:, 0:W], ALU.mult)
            g_col, b_col = affine
            if out_bf is not None:
                p.ts(out_bf[blk][:, 0:W], sc.tmp[:, 0:W], g_col(blk),
                     b_col(blk), ALU.mult, ALU.add)
            if out_f32 is not None:
                p.ts(out_f32[blk][:, 0:W], sc.tmp[:, 0:W], g_col(blk),
                     b_col(blk), ALU.mult, ALU.add)
        else:
            if out_f32 is not None:
                p.tt(out_f32[blk][:, 0:W], sc.tmp[:, 0:W], sc.rstd[:, 0:W],
                     ALU.mult)
                if out_bf is not None:
                    p.copy(out_bf[blk][:, 0:W], out_f32[blk][:, 0:W],
                           engine="gpsimd")
            elif out_bf is not None:
                p.tt(out_bf[blk][:, 0:W], sc.tmp[:, 0:W], sc.rstd[:, 0:W],
                     ALU.mult)


HALO1 = 513
W1 = HALO1 + T_CORE
WH = W1 - 1


def emit_b1(p, nc, env, io, pre_ln):
    xin = io["xin"]
    wqkv = io["wqkv"]
    wo = io["wo"]
    qs = io["qs"]
    ks = io["ks"]
    tabs = io["tabs"]
    lmask = io["lmask"]
    xout = io["xout"]
    if pre_ln:
        u0out = io["u0out"]
    id_bf = env.id_bf
    ones_bf = env.ones_bf
    bd64 = env.bd64
    ssc = p.scope()
    wqkv_bf = load_w_bf(p, wqkv, D, 1536, "wqkv", sc=ssc)
    wo_bf = load_w_bf(p, wo, 512, D, "wo", sc=ssc)
    qs_sb = p.sbs(ssc, "qs_sb", [128, 1], F32)
    ks_sb = p.sbs(ssc, "ks_sb", [128, 1], F32)
    p.dma(qs_sb[:], qs.rearrange("(c p) -> p c", p=128),
          allow_slow_non_contiguous=True)
    p.dma(ks_sb[:], ks.rearrange("(c p) -> p c", p=128),
          allow_slow_non_contiguous=True)
    mask_sb = [p.sbs(ssc, f"msk{i}", [128, 512], BF16) for i in range(8)]
    o_all = [p.sbs(ssc, f"oall{i}", [128, T_CORE], BF16) for i in range(4)]
    tab_cs = p.sbs(ssc, "tab_cs", [128, 512], F32)
    tab_ss = p.sbs(ssc, "tab_ss", [128, 512], F32)
    rn_sb = p.sbs(ssc, "rn_sb", [128, 512], F32)
    sq_bf = p.sbs(ssc, "sq_bf", [128, 512], BF16)
    sw_bf = p.sbs(ssc, "sw_bf", [128, 512], BF16)
    vstore = p.sbs(ssc, "vstore", [128, 130], BF16)
    p.memset(vstore[:, 64:65], 1.0, engine="gpsimd")
    p.memset(vstore[:, 129:130], 1.0, engine="gpsimd")
    p_bf0 = p.sbs(ssc, "p_bf0", [128, 512], BF16)
    p_bf1 = p.sbs(ssc, "p_bf1", [128, 512], BF16)
    d_sb = p.sbs(ssc, "d_sb", [1, 512], F32)
    db_sb = p.sbs(ssc, "db_sb", [64, 512], F32)
    ones1_f = env.ones1_f
    res_sb = p.sbs(ssc, "res_sb", [128, 512], F32)
    out_sb = p.sbs(ssc, "out_sb", [128, 512], F32)

    ps_a = p.pss(ssc, "ps_a", [128, 512], F32)
    ps_b = p.pss(ssc, "ps_b", [128, 512], F32)
    ps_tr = p.pss(ssc, "ps_tr", [128, 128], BF16)
    ps_o0 = p.pss(ssc, "ps_o0", [65, 512], F32)
    ps_o1 = p.pss(ssc, "ps_o1", [65, 512], F32)
    ps_bc = p.pss(ssc, "ps_bc", [64, 512], F32)

    for b in range(B):
        bat_sc = p.scope()
        xc = [p.sbs(bat_sc, f"xc{i}_{b}", [128, W1], BF16) for i in range(8)]
        x_sc = p.scope()
        x8 = [p.sbs(x_sc, f"x{i}_{b}", [128, W1], F32) for i in range(8)]
        ln_sc = p.scope()
        lns = LNScratch2(p, ln_sc, ps_a, ps_b)
        for blk in range(8):
            p.dma(x8[blk][:], xin[b, blk * 128:(blk + 1) * 128, :])
        if pre_ln:
            emit_ln2(p, x8, W1, lns, ones_bf, out_f32=x8)
            for blk in range(8):
                p.dma(u0out[b, blk * 128:(blk + 1) * 128, :],
                      x8[blk][:, HALO1:W1])
        for blk in range(8):
            p.copy(xc[blk][:, :], x8[blk][:, :], engine="gpsimd")
        ln_sc.close()
        x_sc.close()

        def hsl(blk, c0, cw):
            if blk < 4:
                return xc[blk][:, c0 + 1:c0 + 1 + cw]
            return xc[blk][:, c0:c0 + cw]

        qkv_sc = p.scope()
        qkv_bf = [p.sbs(qkv_sc, f"qkv{i}_{b}", [128, WH], BF16)
                  for i in range(12)]
        for m in range(12):
            for c in range(5):
                c0 = c * 512
                for k in range(8):
                    p.mm(ps_a[:, :], wqkv_bf[k][:, m * 128:(m + 1) * 128],
                         hsl(k, c0, 512), start=(k == 0), stop=(k == 7))
                p.copy(qkv_bf[m][:, c0:c0 + 512], ps_a[:, :])

        for part, scol, tq in ((0, qs_sb, 0), (4, ks_sb, 2)):
            for c in range(5):
                c0 = c * 512
                p.dma(tab_cs[:], tabs[tq, :, c0:c0 + 512])
                p.dma(tab_ss[:], tabs[tq + 1, :, c0:c0 + 512])
                for j in range(4):
                    t = qkv_bf[part + j]
                    p.act(sq_bf[:, :], t[:, c0:c0 + 512], AF.Square)
                    p.mm(ps_a[:, :], bd64[:, :], sq_bf[:, :],
                         start=True, stop=True)
                    p.act(rn_sb[:, :], ps_a[:, :], AF.Sqrt)
                    p.ts(rn_sb[:, :], rn_sb[:, :], p.cvec(1e-12), None,
                         ALU.max)
                    p.recip(rn_sb[:, :], rn_sb[:, :])
                    p.stt(t[:, c0:c0 + 512], t[:, c0:c0 + 512],
                          scol[:, 0:1], rn_sb[:, :], ALU.mult, ALU.mult)
                    for half in range(4):
                        pr, psrc = half * 32, (half ^ 1) * 32
                        p.copy(sw_bf[pr:pr + 32, :],
                               t[psrc:psrc + 32, c0:c0 + 512])
                    p.tt(t[:, c0:c0 + 512], t[:, c0:c0 + 512],
                         tab_cs[:, :], ALU.mult)
                    p.tt(sw_bf[:, :], sw_bf[:, :], tab_ss[:, :], ALU.mult)
                    p.tt(t[:, c0:c0 + 512], t[:, c0:c0 + 512], sw_bf[:, :],
                         ALU.add)

        for w in range(4):
            for i in range(8):
                p.dma(mask_sb[i][:], lmask[w, i, :, :])
            q0 = 512 + 512 * w
            for hp in range(4):
                qt = qkv_bf[hp]
                kt = qkv_bf[4 + hp]
                vt = qkv_bf[8 + hp]
                for kti in range(8):
                    k0 = 512 * w + kti * 128
                    p.transpose(ps_tr[:, :], vt[:, k0:k0 + 128], id_bf[:])
                    p.copy(vstore[:, 0:64], ps_tr[:, 0:64])
                    p.copy(vstore[:, 65:129], ps_tr[:, 64:128])
                    p.mm(ps_a[:, :], kt[0:64, k0:k0 + 128],
                         qt[0:64, q0:q0 + 512], start=True, stop=True,
                         tile_position=(0, 0))
                    p.mm(ps_b[:, :], kt[64:128, k0:k0 + 128],
                         qt[64:128, q0:q0 + 512], start=True, stop=True,
                         tile_position=(64, 0))
                    p.act(p_bf0[:, :], ps_a[:, :], AF.Exp,
                          bias=p.cvec(-20.0), scale=8.0)
                    p.act(p_bf1[:, :], ps_b[:, :], AF.Exp,
                          bias=p.cvec(-20.0), scale=8.0)
                    p.tt(p_bf0[:, :], p_bf0[:, :], mask_sb[kti][:, :],
                         ALU.mult)
                    p.tt(p_bf1[:, :], p_bf1[:, :], mask_sb[kti][:, :],
                         ALU.mult)
                    p.mm(ps_o0[:, :], vstore[:, 0:65], p_bf0[:, :],
                         start=(kti == 0), stop=(kti == 7))
                    p.mm(ps_o1[:, :], vstore[:, 65:130], p_bf1[:, :],
                         start=(kti == 0), stop=(kti == 7))
                oc0 = q0 - 512
                for hi, ps_o in ((0, ps_o0), (1, ps_o1)):
                    p.recip(d_sb[:, :], ps_o[64:65, :])
                    p.mm(ps_bc[:, :], ones1_f[:, :], d_sb[:, :],
                         start=True, stop=True)
                    p.copy(db_sb[:, :], ps_bc[:, :])
                    p.tt(o_all[hp][hi * 64:(hi + 1) * 64, oc0:oc0 + 512],
                         ps_o[0:64, :], db_sb[:, :], ALU.mult)
        qkv_sc.close()
        bat_sc.close()

        usrc = u0out if pre_ln else xin
        uoff = 0 if pre_ln else HALO1
        for m in range(8):
            for c in range(4):
                c0 = c * 512
                for k in range(4):
                    p.mm(ps_a[:, :], wo_bf[k][:, m * 128:(m + 1) * 128],
                         o_all[k][:, c0:c0 + 512],
                         start=(k == 0), stop=(k == 3))
                p.dma(res_sb[:, :],
                      usrc[b, m * 128:(m + 1) * 128,
                           uoff + c0:uoff + c0 + 512])
                p.tt(out_sb[:, :], res_sb[:, :], ps_a[:, :], ALU.add)
                p.dma(xout[b, m * 128:(m + 1) * 128, c0:c0 + 512],
                      out_sb[:, :])
    ssc.close()


def build_b1(pre_ln):
    nc = bass.Bass("TRN2", target_bir_lowering=False)
    io = {
        "xin": decl_in(nc, "xin", [B, D, W1]),
        "wqkv": decl_in(nc, "wqkv", [D, 1536]),
        "wo": decl_in(nc, "wo", [512, D]),
        "qs": decl_in(nc, "qs", [128]),
        "ks": decl_in(nc, "ks", [128]),
        "tabs": decl_in(nc, "tabs", [4, 128, WH]),
        "lmask": decl_in_bf(nc, "lmask", [4, 8, 128, 512]),
        "xout": decl_out(nc, "xout", [B, D, T_CORE]),
    }
    ident = decl_in(nc, "ident", [128, 128])
    if pre_ln:
        io["u0out"] = decl_out(nc, "u0out", [B, D, T_CORE])
    p = Prog(nc)
    env = Env(p, ident)
    emit_b1(p, nc, env, io, pre_ln)
    p.finish()
    return nc


class Env:
    def __init__(self, p, ident_ap):
        self.id_f = p.sb("id_f", [128, 128], F32)
        self.id_bf = p.sb("id_bf", [128, 128], BF16)
        p.dma(self.id_f[:], ident_ap)
        p.copy(self.id_bf[:], self.id_f[:], engine="gpsimd")
        self.ones_bf = p.sb("ones_bf", [128, 128], BF16)
        p.memset(self.ones_bf[:], 1.0, engine="gpsimd")
        self.bd64 = p.sb("bd64", [128, 128], BF16)
        p.memset(self.bd64[:], 0.0, engine="gpsimd")
        p.memset(self.bd64[0:64, 0:64], 1.0, engine="gpsimd")
        p.memset(self.bd64[64:128, 64:128], 1.0, engine="gpsimd")
        self.ones1_f = p.sb("ones1_f", [1, 64], F32)
        p.memset(self.ones1_f[:], 1.0, engine="gpsimd")
        for v in (1e-5, 1e-12, -20.0, 0.5):
            p.cvec(v)


_LN_CNT = [0]


class LNScratch2:
    def __init__(self, p, scope, ps_s, ps_q):
        n = _LN_CNT[0]; _LN_CNT[0] += 1
        self.mean = p.sbs(scope, f"ln_mean{n}", [128, 512], F32)
        self.rstd = p.sbs(scope, f"ln_rstd{n}", [128, 512], F32)
        self.tmp = p.sbs(scope, f"ln_tmp{n}", [128, 512], F32)
        self.xb = p.sbs(scope, f"ln_xb{n}", [128, 512], BF16)
        self.ps_s = ps_s
        self.ps_q = ps_q


def emit_ln2(p, x_tiles, W, sc, ones_bf, out_bf=None, out_f32=None,
             affine=None):
    if callable(x_tiles):
        xs = x_tiles
    else:
        xs = lambda blk, c0, cw: x_tiles[blk][:, c0:c0 + cw]
    nch = (W + 511) // 512
    for c in range(nch):
        c0 = c * 512
        cw = min(512, W - c0)
        for blk in range(8):
            p.copy(sc.xb[:, 0:cw], xs(blk, c0, cw), engine="gpsimd")
            p.mm(sc.ps_s[:, 0:cw], ones_bf[:, :], sc.xb[:, 0:cw],
                 start=(blk == 0), stop=(blk == 7))
            p.act(sc.xb[:, 0:cw], xs(blk, c0, cw), AF.Square)
            p.mm(sc.ps_q[:, 0:cw], ones_bf[:, :], sc.xb[:, 0:cw],
                 start=(blk == 0), stop=(blk == 7))
        p.act(sc.mean[:, 0:cw], sc.ps_s[:, 0:cw], AF.Copy,
              scale=1.0 / 1024.0)
        p.act(sc.rstd[:, 0:cw], sc.ps_q[:, 0:cw], AF.Copy,
              scale=1.0 / 1024.0)
        p.tt(sc.tmp[:, 0:cw], sc.mean[:, 0:cw], sc.mean[:, 0:cw], ALU.mult)
        p.tt(sc.rstd[:, 0:cw], sc.rstd[:, 0:cw], sc.tmp[:, 0:cw],
             ALU.subtract)
        p.act(sc.rstd[:, 0:cw], sc.rstd[:, 0:cw], AF.Sqrt,
              bias=p.cvec(1e-5))
        p.recip(sc.rstd[:, 0:cw], sc.rstd[:, 0:cw])
        for blk in range(8):
            p.tt(sc.tmp[:, 0:cw], xs(blk, c0, cw), sc.mean[:, 0:cw],
                 ALU.subtract)
            if affine is not None:
                g_col, b_col = affine
                p.tt(sc.tmp[:, 0:cw], sc.tmp[:, 0:cw], sc.rstd[:, 0:cw],
                     ALU.mult)
                if out_bf is not None:
                    p.ts(out_bf[blk][:, c0:c0 + cw], sc.tmp[:, 0:cw],
                         g_col(blk), b_col(blk), ALU.mult, ALU.add)
                if out_f32 is not None:
                    p.ts(out_f32[blk][:, c0:c0 + cw], sc.tmp[:, 0:cw],
                         g_col(blk), b_col(blk), ALU.mult, ALU.add)
            else:
                if out_f32 is not None:
                    p.tt(out_f32[blk][:, c0:c0 + cw], sc.tmp[:, 0:cw],
                         sc.rstd[:, 0:cw], ALU.mult)
                    if out_bf is not None:
                        p.copy(out_bf[blk][:, c0:c0 + cw],
                               out_f32[blk][:, c0:c0 + cw], engine="gpsimd")
                elif out_bf is not None:
                    p.tt(out_bf[blk][:, c0:c0 + cw], sc.tmp[:, 0:cw],
                         sc.rstd[:, 0:cw], ALU.mult)


DIM_HEAD = 64
WINDOW = 512


def _rotary_tables(t0, ncols):
    half = np.arange(0, DIM_HEAD, 2, dtype=np.float64)
    inv_freq = 1.0 / 10000.0 ** (half / DIM_HEAD)
    t = (np.arange(ncols, dtype=np.float64) + t0)
    freqs = t[:, None] * inv_freq[None, :]
    base = (half + 0.4 * DIM_HEAD) / (1.4 * DIM_HEAD)
    power = (t - T_FULL // 2) / float(WINDOW // 2)
    scale = base[None, :] ** power[:, None]
    cos, sin = np.cos(freqs), np.sin(freqs)
    cs_q = (cos * scale).T
    ss_q = (sin * scale).T
    cs_k = (cos / scale).T
    ss_k = (sin / scale).T

    def pack(cs, ss):
        CS = np.concatenate([cs, cs], axis=0)
        SS = np.concatenate([-ss, ss], axis=0)
        CS = np.concatenate([CS, CS], axis=0)
        SS = np.concatenate([SS, SS], axis=0)
        return CS, SS
    CSq, SSq = pack(cs_q, ss_q)
    CSk, SSk = pack(cs_k, ss_k)
    return np.stack([CSq, SSq, CSk, SSk]).astype(np.float32)


def _local_masks(core):
    out = np.zeros((4, 8, 128, 512), dtype=np.float32)
    p = np.arange(128)[:, None]
    c = np.arange(512)[None, :]
    for w in range(4):
        qglob0 = core * T_CORE + w * 512
        for kti in range(8):
            rel = kti * 128 - 512
            d = rel + p - c
            keep = (d <= 0) & (d >= -WINDOW)
            jglob = qglob0 + rel + p + 0 * c
            keep = keep & (jglob >= 0)
            out[w, kti] = keep
    return out.astype(BF16_NP)


def _band_masks(core):
    out = np.zeros((4, 6, 128, 512), dtype=np.float32)
    p = np.arange(128)[:, None]
    c = np.arange(512)[None, :]
    for qt in range(4):
        qglob0 = core * T_CORE + qt * 512
        for kti in range(6):
            rel = kti * 128 - 256
            d = rel + p - c
            keep = (d <= 0) & (d >= -255)
            jglob = qglob0 + rel + p + 0 * c
            keep = keep & (jglob >= 0)
            out[qt, kti] = keep / 512.0
    return out.astype(BF16_NP)


def _causal_masks():
    out = np.zeros((4, 128, 512), dtype=np.float32)
    p = np.arange(128)[:, None]
    c = np.arange(512)[None, :]
    for i in range(4):
        rel = i * 128
        out[i] = (rel + p) <= c
    return out.astype(BF16_NP)


def _tok_slices(x_fm, halo):
    xp = np.pad(x_fm, ((0, 0), (0, 0), (halo, 0)))
    return [np.ascontiguousarray(xp[:, :, c * T_CORE:c * T_CORE + halo
                                    + T_CORE]) for c in range(N_CORES)]


_IDENT = np.eye(128, dtype=np.float32)


HID = 2730
NJ = 22


def _jsz(j):
    return 128 if j < 21 else HID - 21 * 128


def emit_b2(p, nc, env, io, combine):
    xin = io["xin"]
    w1 = io["w1"]
    b1a = io["b1a"]
    b1g = io["b1g"]
    w2 = io["w2"]
    b2 = io["b2"]
    lng = io["lng"]
    lnb = io["lnb"]
    if combine:
        xorig = io["xorig"]
    xout = io["xout"]
    ones_bf = env.ones_bf
    ssc = p.scope()
    w1_bf = load_w_bf(p, w1, D, 2 * HID, "w1", sc=ssc)
    w2_bf = load_w_bf(p, w2, HID, D, "w2", sc=ssc)
    b1ac = load_bias_col(p, b1a, NJ * 128, "b1ac", sc=ssc)
    b1gc = load_bias_col(p, b1g, NJ * 128, "b1gc", sc=ssc)
    b2c = load_bias_col(p, b2, D, "b2c", sc=ssc)
    lngc = load_bias_col(p, lng, D, "lngc", sc=ssc)
    lnbc = load_bias_col(p, lnb, D, "lnbc", sc=ssc)
    hn_bf = [p.sbs(ssc, f"hn{i}", [128, 512], BF16) for i in range(8)]
    glu_bf = [p.sbs(ssc, f"glu{i}", [128, 512], BF16) for i in range(NJ)]
    gg_bf = p.sbs(ssc, "gg_bf", [128, 512], BF16)
    a_sb = p.sbs(ssc, "a_sb", [128, 512], F32)
    t_sb = p.sbs(ssc, "t_sb", [128, 512], F32)
    r_sb = p.sbs(ssc, "r_sb", [128, 512], F32)
    out_sb = p.sbs(ssc, "out_sb", [128, 512], F32)
    ps_a = p.pss(ssc, "ps_a", [128, 512], F32)
    ps_b = p.pss(ssc, "ps_b", [128, 512], F32)
    ps_s = p.pss(ssc, "ps_s", [128, 512], F32)
    ps_q = p.pss(ssc, "ps_q", [128, 512], F32)

    for b in range(B):
        for c in range(4):
            c0 = c * 512
            ch_sc = p.scope()
            x9 = [p.sbs(ch_sc, f"x{i}_{b}_{c}", [128, 513], F32)
                  for i in range(8)]
            for blk in range(8):
                p.dma(x9[blk][:], xin[b, blk * 128:(blk + 1) * 128,
                                      c0:c0 + 513])

            def hsl(blk, cc0, cw):
                off = 1 if blk < 4 else 0
                return x9[blk][:, off + cc0:off + cc0 + cw]

            ln_sc = p.scope()
            lns = LNScratch2(p, ln_sc, ps_s, ps_q)
            emit_ln2(p, hsl, 512, lns, ones_bf, out_bf=hn_bf,
                     affine=(lambda blk: lngc[:, blk:blk + 1],
                             lambda blk: lnbc[:, blk:blk + 1]))
            ln_sc.close()

            for j in range(NJ):
                sz = _jsz(j)
                for k in range(8):
                    p.mm(ps_a[0:sz, :], w1_bf[k][:, j * 128:j * 128 + sz],
                         hn_bf[k][:, :], start=(k == 0), stop=(k == 7))
                for k in range(8):
                    p.mm(ps_b[0:sz, :],
                         w1_bf[k][:, HID + j * 128:HID + j * 128 + sz],
                         hn_bf[k][:, :], start=(k == 0), stop=(k == 7))
                p.act(gg_bf[0:sz, :], ps_b[0:sz, :], AF.Gelu,
                      bias=b1gc[0:sz, j:j + 1])
                p.emit("scalar",
                       lambda e, s=sz, jj=j: e.add(
                           a_sb[0:s, :], ps_a[0:s, :],
                           b1ac[0:s, jj:jj + 1]), 1)
                p.tt(glu_bf[j][0:sz, :], a_sb[0:sz, :], gg_bf[0:sz, :],
                     ALU.mult)

            for m in range(8):
                for j in range(NJ):
                    sz = _jsz(j)
                    p.mm(ps_a[:, :], w2_bf[j][0:sz, m * 128:(m + 1) * 128],
                         glu_bf[j][0:sz, :], start=(j == 0),
                         stop=(j == NJ - 1))
                p.emit("scalar",
                       lambda e, mm=m: e.add(t_sb[:, :], ps_a[:, :],
                                             b2c[:, mm:mm + 1]), 1)
                xsl = x9[m][:, 1:513]
                if combine:
                    p.dma(r_sb[:, :], xorig[b, m * 128:(m + 1) * 128,
                                            c0:c0 + 512])
                    p.stt(out_sb[:, :], xsl, p.cvec(0.5), r_sb[:, :],
                          ALU.mult, ALU.add)
                    p.stt(out_sb[:, :], t_sb[:, :], p.cvec(0.5),
                          out_sb[:, :], ALU.mult, ALU.add)
                else:
                    p.tt(out_sb[:, :], xsl, t_sb[:, :], ALU.add)
                p.dma(xout[b, m * 128:(m + 1) * 128, c0:c0 + 512],
                      out_sb[:, :])
            ch_sc.close()
    ssc.close()


def build_b2(combine):
    nc = bass.Bass("TRN2", target_bir_lowering=False)
    io = {
        "xin": decl_in(nc, "xin", [B, D, T_CORE + 1]),
        "w1": decl_in(nc, "w1", [D, 2 * HID]),
        "b1a": decl_in(nc, "b1a", [NJ * 128]),
        "b1g": decl_in(nc, "b1g", [NJ * 128]),
        "w2": decl_in(nc, "w2", [HID, D]),
        "b2": decl_in(nc, "b2", [D]),
        "lng": decl_in(nc, "lng", [D]),
        "lnb": decl_in(nc, "lnb", [D]),
        "xout": decl_out(nc, "xout", [B, D, T_CORE]),
    }
    ident = decl_in(nc, "ident", [128, 128])
    if combine:
        io["xorig"] = decl_in(nc, "xorig", [B, D, T_CORE])
    p = Prog(nc)
    env = Env(p, ident)
    emit_b2(p, nc, env, io, combine)
    p.finish()
    return nc


HALO3 = 256
W3 = HALO3 + T_CORE


def emit_b3(p, nc, env, io):
    xin = io["xin"]
    wqkv = io["wqkv"]
    bqkv = io["bqkv"]
    lnw = io["lnw"]
    lnb = io["lnb"]
    bmask = io["bmask"]
    xout = io["xout"]
    id_bf = env.id_bf
    ones_bf = env.ones_bf
    ssc = p.scope()
    bqc = load_bias_col(p, bqkv, 3072, "bqc", sc=ssc)
    lnwc = load_bias_col(p, lnw, D, "lnwc", sc=ssc)
    lnbc = load_bias_col(p, lnb, D, "lnbc", sc=ssc)
    r_sb = p.sbs(ssc, "r_sb", [128, 512], F32)
    out_sb = p.sbs(ssc, "out_sb", [128, 512], F32)
    ps_a = p.pss(ssc, "ps_a", [128, 512], F32)
    ps_b = p.pss(ssc, "ps_b", [128, 512], F32)
    ps_tr = p.pss(ssc, "ps_tr", [128, 128], BF16)
    ps_o = [p.pss(ssc, f"ps_o{i}", [128, 512], F32) for i in range(4)]

    for b in range(B):
        qkv_sc = p.scope()
        q_bf = [p.sbs(qkv_sc, f"q{i}_{b}", [128, T_CORE], BF16)
                for i in range(8)]
        k_bf = [p.sbs(qkv_sc, f"k{i}_{b}", [128, W3], BF16)
                for i in range(8)]
        v_bf = [p.sbs(qkv_sc, f"v{i}_{b}", [128, W3], BF16)
                for i in range(8)]
        xn_sc = p.scope()
        xn = [p.sbs(xn_sc, f"xn{i}_{b}", [128, W3], BF16) for i in range(8)]
        x_sc = p.scope()
        x8c = [p.sbs(x_sc, f"xc{i}_{b}", [128, 512], F32) for i in range(8)]
        ln_sc = p.scope()
        lns = LNScratch2(p, ln_sc, ps_a, ps_b)
        nch = (W3 + 511) // 512
        for c in range(nch):
            c0 = c * 512
            cw = min(512, W3 - c0)
            for blk in range(8):
                p.dma(x8c[blk][:, 0:cw],
                      xin[b, blk * 128:(blk + 1) * 128, c0:c0 + cw])
            emit_ln2(p, x8c, cw, lns, ones_bf,
                     out_bf=[t[:, c0:c0 + cw] for t in xn])
        ln_sc.close()
        x_sc.close()

        wqk_sc = p.scope()
        wqk = [p.sbs(wqk_sc, f"wqk{k}_{b}", [128, 2048], BF16)
               for k in range(8)]
        st_sc = p.scope()
        stg = p.sbs(st_sc, f"stg_{b}", [128, 2048], F32)
        for k in range(8):
            p.dma(stg[:], wqkv[k * 128:(k + 1) * 128, 0:2048])
            p.copy(wqk[k][:], stg[:], engine="gpsimd")
        st_sc.close()
        for m in range(16):
            cols = range(4) if m < 8 else range(5)
            for c in cols:
                if m < 8:
                    c0s = HALO3 + c * 512
                    cw = 512
                else:
                    c0s = c * 512
                    cw = min(512, W3 - c0s)
                for k in range(8):
                    p.mm(ps_a[:, 0:cw], wqk[k][:, m * 128:(m + 1) * 128],
                         xn[k][:, c0s:c0s + cw],
                         start=(k == 0), stop=(k == 7))
                dst = (q_bf[m][:, c * 512:c * 512 + cw] if m < 8
                       else k_bf[m - 8][:, c0s:c0s + cw])
                p.addcol(dst, ps_a[:, 0:cw], bqc[:, m:m + 1])
        wqk_sc.close()
        wv_sc = p.scope()
        wv = [p.sbs(wv_sc, f"wv{k}_{b}", [128, 1024], BF16)
              for k in range(8)]
        st_sc = p.scope()
        stg = p.sbs(st_sc, f"stv_{b}", [128, 1024], F32)
        for k in range(8):
            p.dma(stg[:], wqkv[k * 128:(k + 1) * 128, 2048:3072])
            p.copy(wv[k][:], stg[:], engine="gpsimd")
        st_sc.close()
        for m in range(8):
            for c in range(5):
                c0s = c * 512
                cw = min(512, W3 - c0s)
                for k in range(8):
                    p.mm(ps_a[:, 0:cw], wv[k][:, m * 128:(m + 1) * 128],
                         xn[k][:, c0s:c0s + cw],
                         start=(k == 0), stop=(k == 7))
                p.addcol(v_bf[m][:, c0s:c0s + cw], ps_a[:, 0:cw],
                         bqc[:, 16 + m:17 + m])
        wv_sc.close()
        xn_sc.close()

        att_sc = p.scope()
        mask_sb = [p.sbs(att_sc, f"msk{i}_{b}", [128, 512], BF16)
                   for i in range(6)]
        vt_sb = [p.sbs(att_sc, f"vt{i}_{b}", [128, D], BF16)
                 for i in range(6)]
        s_bf = [p.sbs(att_sc, f"sbf{i}_{b}", [128, 512], BF16)
                for i in range(6)]
        o_f = [p.sbs(att_sc, f"of{i}_{b}", [128, 512], F32)
               for i in range(8)]
        for qt in range(4):
            for i in range(6):
                p.dma(mask_sb[i][:], bmask[qt, i, :, :])
            for kti in range(6):
                ktc = qt * 512 + kti * 128
                for k in range(8):
                    p.mm(ps_a[:, :], k_bf[k][:, ktc:ktc + 128],
                         q_bf[k][:, qt * 512:qt * 512 + 512],
                         start=(k == 0), stop=(k == 7))
                p.tt(s_bf[kti][:, :], ps_a[:, :], mask_sb[kti][:, :],
                     ALU.mult)
                for vb in range(8):
                    p.transpose(ps_tr[:, :], v_bf[vb][:, ktc:ktc + 128],
                                id_bf[:])
                    p.copy(vt_sb[kti][:, vb * 128:(vb + 1) * 128],
                           ps_tr[:, :])
            for half in range(2):
                for mi in range(4):
                    mb = half * 4 + mi
                    for kti in range(6):
                        p.mm(ps_o[mi][:, :],
                             vt_sb[kti][:, mb * 128:(mb + 1) * 128],
                             s_bf[kti][:, :], start=(kti == 0),
                             stop=(kti == 5))
                    p.copy(o_f[mb][:, :], ps_o[mi][:, :])
            ln_sc = p.scope()
            lns = LNScratch2(p, ln_sc, ps_a, ps_b)
            emit_ln2(p, o_f, 512, lns, ones_bf, out_f32=o_f,
                     affine=(lambda blk: lnwc[:, blk:blk + 1],
                             lambda blk: lnbc[:, blk:blk + 1]))
            ln_sc.close()
            for m in range(8):
                p.dma(r_sb[:, :], xin[b, m * 128:(m + 1) * 128,
                                      HALO3 + qt * 512:HALO3 + qt * 512
                                      + 512])
                p.tt(out_sb[:, :], r_sb[:, :], o_f[m][:, :], ALU.add)
                p.dma(xout[b, m * 128:(m + 1) * 128,
                           qt * 512:qt * 512 + 512], out_sb[:, :])
        att_sc.close()
        qkv_sc.close()
    ssc.close()


def build_b3():
    nc = bass.Bass("TRN2", target_bir_lowering=False)
    io = {
        "xin": decl_in(nc, "xin", [B, D, W3]),
        "wqkv": decl_in(nc, "wqkv", [D, 3072]),
        "bqkv": decl_in(nc, "bqkv", [3072]),
        "lnw": decl_in(nc, "lnw", [D]),
        "lnb": decl_in(nc, "lnb", [D]),
        "bmask": decl_in_bf(nc, "bmask", [4, 6, 128, 512]),
        "xout": decl_out(nc, "xout", [B, D, T_CORE]),
    }
    ident = decl_in(nc, "ident", [128, 128])
    p = Prog(nc)
    env = Env(p, ident)
    emit_b3(p, nc, env, io)
    p.finish()
    return nc


def emit_b4(p, nc, env, io):
    xin = io["xin"]
    mats = io["mats"]
    bia = io["bia"]
    inc = io["inc"]
    xout = io["xout"]
    ones_bf = env.ones_bf
    ssc = p.scope()
    bcol = {nm: load_bias_col(p, bia[nm], D, nm + "c", sc=ssc)
            for nm in bia}
    incc = load_bias_col(p, inc, D, "incc", sc=ssc)
    xn = [[p.sbs(ssc, f"xn{t}_{i}", [128, T_CORE], BF16) for i in range(8)]
          for t in range(2)]
    g_f = p.sbs(ssc, "g_f", [128, 512], F32)
    g_i = p.sbs(ssc, "g_i", [128, 512], F32)
    g_c = p.sbs(ssc, "g_c", [128, 512], F32)
    t_sb = p.sbs(ssc, "t_sb", [128, 512], F32)
    t2_sb = p.sbs(ssc, "t2_sb", [128, 512], F32)
    r_sb = p.sbs(ssc, "r_sb", [128, 512], F32)
    ps_a = p.pss(ssc, "ps_a", [128, 512], F32)
    ps_b = p.pss(ssc, "ps_b", [128, 512], F32)
    ps_c = p.pss(ssc, "ps_c", [128, 512], F32)
    ps_s = p.pss(ssc, "ps_s", [128, 512], F32)
    ps_q = p.pss(ssc, "ps_q", [128, 512], F32)

    for t in range(2):
        x_sc = p.scope()
        x8c = [p.sbs(x_sc, f"x{t}_{i}", [128, 512], F32) for i in range(8)]
        ln_sc = p.scope()
        lns = LNScratch2(p, ln_sc, ps_s, ps_q)
        for c in range(4):
            c0 = c * 512
            for blk in range(8):
                p.dma(x8c[blk][:], xin[t, blk * 128:(blk + 1) * 128,
                                       c0:c0 + 512])
            emit_ln2(p, x8c, 512, lns, ones_bf,
                     out_bf=[z[:, c0:c0 + 512] for z in xn[t]])
        ln_sc.close()
        x_sc.close()

    c_f = [p.sbs(ssc, f"cf{i}", [128, T_CORE], F32) for i in range(8)]
    cb = [p.sbs(ssc, f"cb{i}", [128, T_CORE], BF16) for i in range(8)]
    for m in range(8):
        p.memset(c_f[m][:], 0.0, engine="gpsimd")
        p.addcol(c_f[m][:], c_f[m][:], incc[:, m:m + 1])
        p.copy(cb[m][:], c_f[m][:], engine="gpsimd")

    _gw_cnt = [0]

    def gate_w(names, m, sc):
        tiles = {}
        n = _gw_cnt[0]
        _gw_cnt[0] += 1
        st = p.sbs(sc, f"gst{n}", [128, 128], F32)
        for nm in names:
            tl = []
            for k in range(8):
                tt_ = p.sbs(sc, f"g{nm}{n}_{k}", [128, 128], BF16)
                p.dma(st[:], mats[nm][k * 128:(k + 1) * 128,
                                      m * 128:(m + 1) * 128])
                p.copy(tt_[:], st[:], engine="gpsimd")
                tl.append(tt_)
            tiles[nm] = tl
        return tiles

    for step in range(2):
        xs = xn[step]
        for m in range(8):
            w_sc = p.scope()
            tw = gate_w(("wf", "uf", "wi", "ui", "wc", "uc"), m, w_sc)
            for c in range(4):
                c0 = c * 512
                for gname, wn, un, psx, gout, fn in (
                        ("f", "wf", "uf", ps_a, g_f, AF.Sigmoid),
                        ("i", "wi", "ui", ps_b, g_i, AF.Sigmoid),
                        ("c", "wc", "uc", ps_c, g_c, AF.Tanh)):
                    for k in range(8):
                        p.mm(psx[:, :], tw[wn][k][:], xs[k][:, c0:c0 + 512],
                             start=(k == 0), stop=False)
                    for k in range(8):
                        p.mm(psx[:, :], tw[un][k][:],
                             cb[k][:, c0:c0 + 512],
                             start=False, stop=(k == 7))
                    p.act(gout[:, :], psx[:, :], fn,
                          bias=bcol["b" + gname][:, m:m + 1])
                p.tt(t_sb[:, :], g_f[:, :], c_f[m][:, c0:c0 + 512],
                     ALU.mult)
                p.tt(t2_sb[:, :], g_i[:, :], g_c[:, :], ALU.mult)
                p.tt(c_f[m][:, c0:c0 + 512], t_sb[:, :], t2_sb[:, :],
                     ALU.add)
            w_sc.close()
        for m in range(8):
            p.copy(cb[m][:], c_f[m][:], engine="gpsimd")

    for t in range(2):
        for m in range(8):
            w_sc = p.scope()
            tw = gate_w(("wo", "uo"), m, w_sc)
            for c in range(4):
                c0 = c * 512
                for k in range(8):
                    p.mm(ps_a[:, :], tw["wo"][k][:],
                         xn[t][k][:, c0:c0 + 512],
                         start=(k == 0), stop=False)
                for k in range(8):
                    p.mm(ps_a[:, :], tw["uo"][k][:],
                         cb[k][:, c0:c0 + 512],
                         start=False, stop=(k == 7))
                p.act(t_sb[:, :], ps_a[:, :], AF.Tanh,
                      bias=bcol["bo"][:, m:m + 1])
                p.dma(r_sb[:, :], xin[t, m * 128:(m + 1) * 128,
                                      c0:c0 + 512])
                p.stt(t2_sb[:, :], t_sb[:, :], p.cvec(0.5), r_sb[:, :],
                      ALU.mult, ALU.add)
                p.dma(xout[t, m * 128:(m + 1) * 128, c0:c0 + 512],
                      t2_sb[:, :])
            w_sc.close()
    ssc.close()


def build_b4():
    nc = bass.Bass("TRN2", target_bir_lowering=False)
    mats = {}
    for nm in ("wf", "uf", "wi", "ui", "wc", "uc", "wo", "uo"):
        mats[nm] = decl_in(nc, nm + "T", [D, D])
    bia = {}
    for nm in ("bf", "bi", "bc", "bo"):
        bia[nm] = decl_in(nc, nm, [D])
    io = {
        "xin": decl_in(nc, "xin", [B, D, T_CORE]),
        "mats": mats, "bia": bia,
        "inc": decl_in(nc, "inc", [D]),
        "xout": decl_out(nc, "xout", [B, D, T_CORE]),
    }
    ident = decl_in(nc, "ident", [128, 128])
    p = Prog(nc)
    env = Env(p, ident)
    emit_b4(p, nc, env, io)
    p.finish()
    return nc


def emit_b5a(p, nc, env, io):
    xin = io["xin"]
    wqT, wkT, wvT = io["wqT"], io["wkT"], io["wvT"]
    bq, bk, bv = io["bq"], io["bk"], io["bv"]
    qf, kf, vt = io["qf"], io["kf"], io["vt"]
    id_bf = env.id_bf
    ones_bf = env.ones_bf
    ssc = p.scope()
    bqc = load_bias_col(p, bq, D, "bqc", sc=ssc)
    bkc = load_bias_col(p, bk, D, "bkc", sc=ssc)
    bvc = load_bias_col(p, bv, D, "bvc", sc=ssc)
    wq_bf = load_w_bf(p, wqT, D, D, "wq", sc=ssc)
    wk_bf = load_w_bf(p, wkT, D, D, "wk", sc=ssc)
    wv_bf = load_w_bf(p, wvT, D, D, "wv", sc=ssc)
    xn = [p.sbs(ssc, f"xn{i}", [128, T_CORE], BF16) for i in range(8)]
    o_bf = p.sbs(ssc, "o_bf", [128, 512], BF16)
    v_m = p.sbs(ssc, "v_m", [128, T_CORE], BF16)
    vt_s = p.sbs(ssc, "vt_s", [128, 128], BF16)
    ps_a = p.pss(ssc, "ps_a", [128, 512], F32)
    ps_s = p.pss(ssc, "ps_s", [128, 512], F32)
    ps_q = p.pss(ssc, "ps_q", [128, 512], F32)
    ps_tr = p.pss(ssc, "ps_tr", [128, 128], BF16)

    for b in range(B):
        x_sc = p.scope()
        x8c = [p.sbs(x_sc, f"x{b}_{i}", [128, 512], F32) for i in range(8)]
        ln_sc = p.scope()
        lns = LNScratch2(p, ln_sc, ps_s, ps_q)
        for c in range(4):
            c0 = c * 512
            for blk in range(8):
                p.dma(x8c[blk][:], xin[b, blk * 128:(blk + 1) * 128,
                                       c0:c0 + 512])
            emit_ln2(p, x8c, 512, lns, ones_bf,
                     out_bf=[z[:, c0:c0 + 512] for z in xn])
        ln_sc.close()
        x_sc.close()
        for w_bf, bc, dst in ((wq_bf, bqc, qf), (wk_bf, bkc, kf)):
            for m in range(8):
                for c in range(4):
                    c0 = c * 512
                    for k in range(8):
                        p.mm(ps_a[:, :], w_bf[k][:, m * 128:(m + 1) * 128],
                             xn[k][:, c0:c0 + 512],
                             start=(k == 0), stop=(k == 7))
                    p.addcol(o_bf[:, :], ps_a[:, :], bc[:, m:m + 1])
                    p.dma(dst[b, m * 128:(m + 1) * 128, c0:c0 + 512],
                          o_bf[:, :])
        for m in range(8):
            for c in range(4):
                c0 = c * 512
                for k in range(8):
                    p.mm(ps_a[:, :], wv_bf[k][:, m * 128:(m + 1) * 128],
                         xn[k][:, c0:c0 + 512],
                         start=(k == 0), stop=(k == 7))
                p.addcol(v_m[:, c0:c0 + 512], ps_a[:, :], bvc[:, m:m + 1])
            for tt in range(16):
                p.transpose(ps_tr[:, :], v_m[:, tt * 128:(tt + 1) * 128],
                            id_bf[:])
                p.copy(vt_s[:, :], ps_tr[:, :])
                p.dma(vt[b, tt * 128:(tt + 1) * 128,
                         m * 128:(m + 1) * 128], vt_s[:, :])
    ssc.close()


def build_b5a():
    nc = bass.Bass("TRN2", target_bir_lowering=False)
    io = {
        "xin": decl_in(nc, "xin", [B, D, T_CORE]),
        "wqT": decl_in(nc, "wqT", [D, D]),
        "wkT": decl_in(nc, "wkT", [D, D]),
        "wvT": decl_in(nc, "wvT", [D, D]),
        "bq": decl_in(nc, "bq", [D]),
        "bk": decl_in(nc, "bk", [D]),
        "bv": decl_in(nc, "bv", [D]),
        "qf": nc.dram_tensor("qf", [B, D, T_CORE], BF16,
                             kind="ExternalOutput").ap(),
        "kf": nc.dram_tensor("kf", [B, D, T_CORE], BF16,
                             kind="ExternalOutput").ap(),
        "vt": nc.dram_tensor("vt", [B, T_CORE, D], BF16,
                             kind="ExternalOutput").ap(),
    }
    ident = decl_in(nc, "ident", [128, 128])
    p = Prog(nc)
    env = Env(p, ident)
    emit_b5a(p, nc, env, io)
    p.finish()
    return nc


NQT = T_FULL // 512


def emit_b5b(p, nc, env, io):
    qh = io["qh"]
    kh = io["kh"]
    vh = io["vh"]
    cmask = io["cmask"]
    oh = io["oh"]
    ones1_f = env.ones1_f
    ssc = p.scope()
    mask_sb = [p.sbs(ssc, f"msk{i}", [128, 512], BF16) for i in range(4)]
    for i in range(4):
        p.dma(mask_sb[i][:], cmask[i, :, :])
    k_res = p.sbs(ssc, "k_res", [128, T_FULL], BF16)
    vt_res = p.sbs(ssc, "vt_res", [128, 130 * 128], BF16)
    p.memset(vt_res[:], 0.0, engine="gpsimd")
    q_sb = p.sbs(ssc, "q_sb", [128, 512], BF16)
    p_bf0 = p.sbs(ssc, "p_bf0", [128, 512], BF16)
    p_bf1 = p.sbs(ssc, "p_bf1", [128, 512], BF16)
    d_sb = p.sbs(ssc, "d_sb", [1, 512], F32)
    db_sb = p.sbs(ssc, "db_sb", [64, 512], F32)
    obuf = p.sbs(ssc, "obuf", [128, 512], BF16)
    ps_s0 = p.pss(ssc, "ps_s0", [128, 512], F32)
    ps_s1 = p.pss(ssc, "ps_s1", [128, 512], F32)
    ps_o0 = p.pss(ssc, "ps_o0", [65, 512], F32)
    ps_o1 = p.pss(ssc, "ps_o1", [65, 512], F32)
    ps_bc = p.pss(ssc, "ps_bc", [64, 512], F32)

    for ti in range(128):
        p.memset(vt_res[:, ti * 130 + 64:ti * 130 + 65], 1.0,
                 engine="gpsimd")
        p.memset(vt_res[:, ti * 130 + 129:ti * 130 + 130], 1.0,
                 engine="gpsimd")

    for u in range(2):
        for qt in range(NQT):
            t0 = qt * 512
            p.dma(k_res[:, t0:t0 + 512], kh[u, :, t0:t0 + 512])
            for ti4 in range(4):
                ti = qt * 4 + ti4
                tt0 = t0 + ti4 * 128
                p.dma(vt_res[:, ti * 130:ti * 130 + 64],
                      vh[u, tt0:tt0 + 128, 0:64])
                p.dma(vt_res[:, ti * 130 + 65:ti * 130 + 129],
                      vh[u, tt0:tt0 + 128, 64:128])
            p.dma(q_sb[:], qh[u, :, t0:t0 + 512])
            nkt = 4 * (qt + 1)
            for kti in range(nkt):
                k0 = kti * 128
                p.mm(ps_s0[:, :], k_res[0:64, k0:k0 + 128], q_sb[0:64, :],
                     start=True, stop=True, tile_position=(0, 0))
                p.mm(ps_s1[:, :], k_res[64:128, k0:k0 + 128],
                     q_sb[64:128, :], start=True, stop=True,
                     tile_position=(64, 0))
                p.act(p_bf0[:, :], ps_s0[:, :], AF.Exp,
                      bias=p.cvec(-20.0), scale=0.125)
                p.act(p_bf1[:, :], ps_s1[:, :], AF.Exp,
                      bias=p.cvec(-20.0), scale=0.125)
                if kti >= nkt - 4:
                    mi = kti - (nkt - 4)
                    p.tt(p_bf0[:, :], p_bf0[:, :], mask_sb[mi][:, :],
                         ALU.mult)
                    p.tt(p_bf1[:, :], p_bf1[:, :], mask_sb[mi][:, :],
                         ALU.mult)
                p.mm(ps_o0[:, :], vt_res[:, kti * 130:kti * 130 + 65],
                     p_bf0[:, :], start=(kti == 0), stop=(kti == nkt - 1))
                p.mm(ps_o1[:, :], vt_res[:, kti * 130 + 65:kti * 130 + 130],
                     p_bf1[:, :], start=(kti == 0), stop=(kti == nkt - 1))
            for hi, ps_o in ((0, ps_o0), (1, ps_o1)):
                p.recip(d_sb[:, :], ps_o[64:65, :])
                p.mm(ps_bc[:, :], ones1_f[:, :], d_sb[:, :],
                     start=True, stop=True)
                p.copy(db_sb[:, :], ps_bc[:, :])
                p.tt(obuf[hi * 64:(hi + 1) * 64, :], ps_o[0:64, :],
                     db_sb[:, :], ALU.mult)
            p.dma(oh[u, :, t0:t0 + 512], obuf[:, :])
    ssc.close()


def build_b5b():
    nc = bass.Bass("TRN2", target_bir_lowering=False)
    io = {
        "qh": decl_in_bf(nc, "qh", [2, 128, T_FULL]),
        "kh": decl_in_bf(nc, "kh", [2, 128, T_FULL]),
        "vh": decl_in_bf(nc, "vh", [2, T_FULL, 128]),
        "cmask": decl_in_bf(nc, "cmask", [4, 128, 512]),
        "oh": nc.dram_tensor("oh", [2, 128, T_FULL], BF16,
                             kind="ExternalOutput").ap(),
    }
    ident = decl_in(nc, "ident", [128, 128])
    p = Prog(nc)
    env = Env(p, ident)
    emit_b5b(p, nc, env, io)
    p.finish()
    return nc


def emit_b5c(p, nc, env, io):
    oin = io["oin"]
    xin = io["xin"]
    woT = io["woT"]
    bo = io["bo"]
    xout = io["xout"]
    ssc = p.scope()
    wo_bf = load_w_bf(p, woT, D, D, "wo", sc=ssc)
    boc = load_bias_col(p, bo, D, "boc", sc=ssc)
    o_bf = [p.sbs(ssc, f"o{i}", [128, T_CORE], BF16) for i in range(8)]
    r_sb = p.sbs(ssc, "r_sb", [128, 512], F32)
    t_sb = p.sbs(ssc, "t_sb", [128, 512], F32)
    out_sb = p.sbs(ssc, "out_sb", [128, 512], F32)
    ps_a = p.pss(ssc, "ps_a", [128, 512], F32)

    for b in range(B):
        for blk in range(8):
            p.dma(o_bf[blk][:], oin[b, blk * 128:(blk + 1) * 128, :])
        for m in range(8):
            for c in range(4):
                c0 = c * 512
                for k in range(8):
                    p.mm(ps_a[:, :], wo_bf[k][:, m * 128:(m + 1) * 128],
                         o_bf[k][:, c0:c0 + 512],
                         start=(k == 0), stop=(k == 7))
                p.addcol(t_sb[:, :], ps_a[:, :], boc[:, m:m + 1])
                p.dma(r_sb[:, :], xin[b, m * 128:(m + 1) * 128,
                                      c0:c0 + 512])
                p.tt(out_sb[:, :], r_sb[:, :], t_sb[:, :], ALU.add)
                p.dma(xout[b, m * 128:(m + 1) * 128, c0:c0 + 512],
                      out_sb[:, :])
    ssc.close()


def build_b5c():
    nc = bass.Bass("TRN2", target_bir_lowering=False)
    io = {
        "oin": decl_in_bf(nc, "oin", [B, D, T_CORE]),
        "xin": decl_in(nc, "xin", [B, D, T_CORE]),
        "woT": decl_in(nc, "woT", [D, D]),
        "bo": decl_in(nc, "bo", [D]),
        "xout": decl_out(nc, "xout", [B, D, T_CORE]),
    }
    ident = decl_in(nc, "ident", [128, 128])
    p = Prog(nc)
    env = Env(p, ident)
    emit_b5c(p, nc, env, io)
    p.finish()
    return nc


_PROGS = {}


def _prog(key, fn, *a):
    if key not in _PROGS:
        _PROGS[key] = fn(*a)
    return _PROGS[key]


def _run(nc, maps):
    return run_bass_kernel_spmd(nc, maps, core_ids=list(range(N_CORES)))


def _gather_fm(res, key):
    return np.concatenate([r[key] for r in res.results], axis=2)


def _l0_ffn_call(x_fm, w, tabs_pc, lmask_pc):
    x0 = x_fm
    u = None
    for l in range(2):
        pre = (l == 0)
        comb = (l == 1)
        src = x0 if pre else u
        xs = _tok_slices(src, HALO1)
        nc = _prog(("b1", pre), build_b1, pre)
        maps = []
        for c in range(N_CORES):
            maps.append({
                "xin": xs[c],
                "wqkv": w["wqkv"][l], "wo": w["wo"][l],
                "qs": np.tile(w["q_scale"][l], 2).astype(np.float32),
                "ks": np.tile(w["k_scale"][l], 2).astype(np.float32),
                "tabs": tabs_pc[c], "lmask": lmask_pc[c], "ident": _IDENT,
            })
        res = _run(nc, maps)
        u = _gather_fm(res, "xout")
        xs2 = _tok_slices(u, 1)
        nc2 = _prog(("b2", comb), build_b2, comb)
        b1v = w["b1"][l]
        b1a = np.zeros(NJ * 128, np.float32)
        b1a[:HID] = b1v[:HID]
        b1g = np.zeros(NJ * 128, np.float32)
        b1g[:HID] = b1v[HID:]
        maps2 = []
        for c in range(N_CORES):
            m = {"xin": xs2[c], "w1": w["w1"][l], "b1a": b1a, "b1g": b1g,
                 "w2": w["w2"][l], "b2": w["b2"][l], "lng": w["ln_g"][l],
                 "lnb": w["ln_b"][l], "ident": _IDENT}
            if comb:
                m["xorig"] = np.ascontiguousarray(
                    x0[:, :, c * T_CORE:(c + 1) * T_CORE])
            maps2.append(m)
        res2 = _run(nc2, maps2)
        u = _gather_fm(res2, "xout")
    return u


def kernel(**inp):
    x_fm = np.ascontiguousarray(
        np.asarray(inp["L0_attn_x"]).transpose(0, 2, 1))

    tabs_pc = [_rotary_tables(c * T_CORE - (HALO1 - 1), W1 - 1)
               for c in range(N_CORES)]
    lmask_pc = [_local_masks(c) for c in range(N_CORES)]
    bmask_pc = [_band_masks(c) for c in range(N_CORES)]
    cmask = _causal_masks()

    l0w = {k: np.asarray(inp["L0_ffn_" + k]) for k in
           ("wqkv", "q_scale", "k_scale", "wo", "ln_g", "ln_b", "w1", "b1",
            "w2", "b2")}

    x_fm = _l0_ffn_call(x_fm, l0w, tabs_pc, lmask_pc)
    xs = _tok_slices(x_fm, HALO3)
    nc = _prog("b3", build_b3)
    maps = [{"xin": xs[c], "wqkv": np.asarray(inp["L0_attn_W_qkv"]),
             "bqkv": np.asarray(inp["L0_attn_b_qkv"]),
             "lnw": np.asarray(inp["L0_attn_ln_w"]),
             "lnb": np.asarray(inp["L0_attn_ln_b"]),
             "bmask": bmask_pc[c], "ident": _IDENT}
            for c in range(N_CORES)]
    x_fm = _gather_fm(_run(nc, maps), "xout")
    x_fm = _l0_ffn_call(x_fm, l0w, tabs_pc, lmask_pc)

    def l1_ffn_call(x_fm):
        xs = _tok_slices(x_fm, 0)
        nc = _prog("b4", build_b4)
        maps = []
        for c in range(N_CORES):
            m = {"xin": xs[c], "inc": np.asarray(inp["L1_ffn_in_c"]),
                 "ident": _IDENT}
            for nm in ("wf", "uf", "wi", "ui", "wc", "uc", "wo", "uo"):
                m[nm + "T"] = np.ascontiguousarray(
                    np.asarray(inp["L1_ffn_" + nm]).T)
            for nm in ("bf", "bi", "bc", "bo"):
                m[nm] = np.asarray(inp["L1_ffn_" + nm])
            maps.append(m)
        return _gather_fm(_run(nc, maps), "xout")

    x_fm = l1_ffn_call(x_fm)

    xs = _tok_slices(x_fm, 0)
    nc = _prog("b5a", build_b5a)
    maps = [{"xin": xs[c],
             "wqT": np.ascontiguousarray(np.asarray(inp["L1_attn_Wq"]).T),
             "wkT": np.ascontiguousarray(np.asarray(inp["L1_attn_Wk"]).T),
             "wvT": np.ascontiguousarray(np.asarray(inp["L1_attn_Wv"]).T),
             "bq": np.asarray(inp["L1_attn_bq"]),
             "bk": np.asarray(inp["L1_attn_bk"]),
             "bv": np.asarray(inp["L1_attn_bv"]),
             "ident": _IDENT}
            for c in range(N_CORES)]
    res = _run(nc, maps)
    qf = np.concatenate([r["qf"] for r in res.results], axis=2)
    kf = np.concatenate([r["kf"] for r in res.results], axis=2)
    vt = np.concatenate([r["vt"] for r in res.results], axis=1)

    units = [(b, hp) for b in range(B) for hp in range(8)]
    nc = _prog("b5b", build_b5b)
    maps = []
    for c in range(N_CORES):
        us = units[2 * c:2 * c + 2]
        maps.append({
            "qh": np.stack([qf[b, hp * 128:(hp + 1) * 128, :]
                            for b, hp in us]),
            "kh": np.stack([kf[b, hp * 128:(hp + 1) * 128, :]
                            for b, hp in us]),
            "vh": np.stack([vt[b, :, hp * 128:(hp + 1) * 128]
                            for b, hp in us]),
            "cmask": cmask, "ident": _IDENT,
        })
    res = _run(nc, maps)
    o_fm = np.zeros((B, D, T_FULL), BF16_NP)
    for c in range(N_CORES):
        for i, (b, hp) in enumerate(units[2 * c:2 * c + 2]):
            o_fm[b, hp * 128:(hp + 1) * 128, :] =                 np.asarray(res.results[c]["oh"][i]).view(BF16_NP)                 if res.results[c]["oh"].dtype != BF16_NP                 else res.results[c]["oh"][i]

    nc = _prog("b5c", build_b5c)
    maps = [{"oin": np.ascontiguousarray(
                 o_fm[:, :, c * T_CORE:(c + 1) * T_CORE]),
             "xin": np.ascontiguousarray(
                 x_fm[:, :, c * T_CORE:(c + 1) * T_CORE]),
             "woT": np.ascontiguousarray(np.asarray(inp["L1_attn_Wo"]).T),
             "bo": np.asarray(inp["L1_attn_bo"]), "ident": _IDENT}
            for c in range(N_CORES)]
    x_fm = _gather_fm(_run(nc, maps), "xout")

    x_fm = l1_ffn_call(x_fm)

    return np.ascontiguousarray(x_fm.transpose(0, 2, 1))


def build_fused():
    nc = bass.Bass("TRN2", target_bir_lowering=False)
    x0 = decl_in(nc, "x0", [B, D, W1])
    zstrip = decl_in(nc, "zstrip", [B * D, HALO1])
    ident = decl_in(nc, "ident", [128, 128])
    tabs = decl_in(nc, "tabs", [4, 128, WH])
    lmask = decl_in_bf(nc, "lmask", [4, 8, 128, 512])
    bmask = decl_in_bf(nc, "bmask", [4, 6, 128, 512])
    cmask = decl_in_bf(nc, "cmask", [4, 128, 512])
    w = {}
    for l in range(2):
        w[f"wqkv{l}"] = decl_in(nc, f"wqkv{l}", [D, 1536])
        w[f"wo{l}"] = decl_in(nc, f"wo{l}", [512, D])
        w[f"qs{l}"] = decl_in(nc, f"qs{l}", [128])
        w[f"ks{l}"] = decl_in(nc, f"ks{l}", [128])
        w[f"w1{l}"] = decl_in(nc, f"w1{l}", [D, 2 * HID])
        w[f"b1a{l}"] = decl_in(nc, f"b1a{l}", [NJ * 128])
        w[f"b1g{l}"] = decl_in(nc, f"b1g{l}", [NJ * 128])
        w[f"w2{l}"] = decl_in(nc, f"w2{l}", [HID, D])
        w[f"b2{l}"] = decl_in(nc, f"b2{l}", [D])
        w[f"lng{l}"] = decl_in(nc, f"lng{l}", [D])
        w[f"lnb{l}"] = decl_in(nc, f"lnb{l}", [D])
    w["wqkv3"] = decl_in(nc, "wqkv3", [D, 3072])
    w["bqkv3"] = decl_in(nc, "bqkv3", [3072])
    w["lnw3"] = decl_in(nc, "lnw3", [D])
    w["lnb3"] = decl_in(nc, "lnb3", [D])
    mats = {nm: decl_in(nc, nm + "T", [D, D])
            for nm in ("wf", "uf", "wi", "ui", "wc", "uc", "wo", "uo")}
    bia = {nm: decl_in(nc, nm, [D]) for nm in ("bf", "bi", "bc", "bo")}
    inc = decl_in(nc, "inc", [D])
    for nm in ("wqT", "wkT", "wvT", "woT5"):
        w[nm] = decl_in(nc, nm, [D, D])
    for nm in ("bq5", "bk5", "bv5", "bo5"):
        w[nm] = decl_in(nc, nm, [D])
    xout_f = decl_out(nc, "xout", [B, D, T_CORE])

    bufA = nc.dram_tensor("bufA", [B, D, W1], F32).ap()
    bufB = nc.dram_tensor("bufB", [B, D, W1], F32).ap()
    bufC = nc.dram_tensor("bufC", [B, D, W1], F32).ap()
    bufD = nc.dram_tensor("bufD", [B, D, W1], F32).ap()
    ud = nc.dram_tensor("ud", [B, D, T_CORE], F32).ap()
    strip = nc.dram_tensor("strip", [B * D, HALO1], F32).ap()
    hg = nc.dram_tensor("hg", [N_CORES * B * D, HALO1], F32).ap()
    qg_in = nc.dram_tensor("qg_in", [B * D, T_CORE], BF16).ap()
    kg_in = nc.dram_tensor("kg_in", [B * D, T_CORE], BF16).ap()
    vg_in = nc.dram_tensor("vg_in", [B * T_CORE, D], BF16).ap()
    qg = nc.dram_tensor("qg", [N_CORES * B * D, T_CORE], BF16).ap()
    kg = nc.dram_tensor("kg", [N_CORES * B * D, T_CORE], BF16).ap()
    vg = nc.dram_tensor("vg", [N_CORES * B * T_CORE, D], BF16).ap()
    qunit = nc.dram_tensor("qunit", [2, 128, T_FULL], BF16).ap()
    kunit = nc.dram_tensor("kunit", [2, 128, T_FULL], BF16).ap()
    vunit = nc.dram_tensor("vunit", [2, T_FULL, 128], BF16).ap()
    oh_int = nc.dram_tensor("oh_int", [2, 128, T_FULL], BF16).ap()
    og_in = nc.dram_tensor("og_in", [N_CORES * 2 * 128, T_CORE], BF16).ap()
    og_all = nc.dram_tensor("og_all",
                            [N_CORES * N_CORES * 2 * 128, T_CORE],
                            BF16).ap()
    o_own = nc.dram_tensor("o_own", [B * D, T_CORE], BF16).ap()

    p = Prog(nc)
    env = Env(p, ident)

    def own(buf):
        return buf[:, :, HALO1:W1]

    def exchange(buf):
        p.dma(strip.rearrange("(b d) h -> b d h", d=D),
              buf[:, :, W1 - HALO1:W1])
        p.collective("AllGather", [strip], [hg])
        hgv = hg.rearrange("(r bd) h -> r bd h", r=N_CORES)
        bufv = buf.rearrange("b d h -> (b d) h")

        def fn(role):
            src = (hgv[role - 1] if role > 0 else zstrip)
            return [(bufv[:, 0:HALO1], src)]
        p.switch_dmas(fn, 1)

    def l0f_call(cin, cout, tmp1, tmp2):
        io = {"xin": cin, "wqkv": w["wqkv0"], "wo": w["wo0"],
              "qs": w["qs0"], "ks": w["ks0"], "tabs": tabs,
              "lmask": lmask, "xout": own(tmp1), "u0out": ud}
        emit_b1(p, nc, env, io, True)
        exchange(tmp1)
        io = {"xin": tmp1[:, :, HALO1 - 1:W1], "w1": w["w10"],
              "b1a": w["b1a0"], "b1g": w["b1g0"], "w2": w["w20"],
              "b2": w["b20"], "lng": w["lng0"], "lnb": w["lnb0"],
              "xout": own(tmp2)}
        emit_b2(p, nc, env, io, False)
        exchange(tmp2)
        io = {"xin": tmp2, "wqkv": w["wqkv1"], "wo": w["wo1"],
              "qs": w["qs1"], "ks": w["ks1"], "tabs": tabs,
              "lmask": lmask, "xout": own(tmp1)}
        emit_b1(p, nc, env, io, False)
        exchange(tmp1)
        io = {"xin": tmp1[:, :, HALO1 - 1:W1], "w1": w["w11"],
              "b1a": w["b1a1"], "b1g": w["b1g1"], "w2": w["w21"],
              "b2": w["b21"], "lng": w["lng1"], "lnb": w["lnb1"],
              "xorig": own(cin), "xout": own(cout)}
        emit_b2(p, nc, env, io, True)

    l0f_call(x0, bufC, bufA, bufB)
    exchange(bufC)
    io = {"xin": bufC[:, :, HALO1 - HALO3:W1], "wqkv": w["wqkv3"],
          "bqkv": w["bqkv3"], "lnw": w["lnw3"], "lnb": w["lnb3"],
          "bmask": bmask, "xout": own(bufA)}
    emit_b3(p, nc, env, io)
    exchange(bufA)
    l0f_call(bufA, bufC, bufB, bufD)

    io = {"xin": own(bufC), "mats": mats, "bia": bia, "inc": inc,
          "xout": own(bufA)}
    emit_b4(p, nc, env, io)
    io = {"xin": own(bufA),
          "wqT": w["wqT"], "wkT": w["wkT"], "wvT": w["wvT"],
          "bq": w["bq5"], "bk": w["bk5"], "bv": w["bv5"],
          "qf": qg_in.rearrange("(b d) t -> b d t", b=B),
          "kf": kg_in.rearrange("(b d) t -> b d t", b=B),
          "vt": vg_in.rearrange("(b t) d -> b t d", b=B)}
    emit_b5a(p, nc, env, io)
    p.collective("AllGather", [qg_in], [qg])
    p.collective("AllGather", [kg_in], [kg])
    p.collective("AllGather", [vg_in], [vg])

    qgv = qg.rearrange("(r b d) t -> r b d t", r=N_CORES, b=B)
    kgv = kg.rearrange("(r b d) t -> r b d t", r=N_CORES, b=B)
    vgv = vg.rearrange("(r b t) d -> r b t d", r=N_CORES, b=B)
    quv = qunit.rearrange("u p (r c) -> u r p c", c=T_CORE)
    kuv = kunit.rearrange("u p (r c) -> u r p c", c=T_CORE)
    vuv = vunit.rearrange("u (r c) p -> u r c p", c=T_CORE)

    def unit_fn(role):
        pairs = []
        for i in range(2):
            uidx = 2 * role + i
            bb, hp = uidx // 8, uidx % 8
            pairs.append((quv[i], qgv[:, bb, hp * 128:(hp + 1) * 128, :]))
            pairs.append((kuv[i], kgv[:, bb, hp * 128:(hp + 1) * 128, :]))
            for r in range(N_CORES):
                pairs.append((vuv[i, r],
                              vgv[r, bb, :, hp * 128:(hp + 1) * 128]))
        return pairs
    p.switch_dmas(unit_fn, 20)

    io = {"qh": qunit, "kh": kunit, "vh": vunit, "cmask": cmask,
          "oh": oh_int}
    emit_b5b(p, nc, env, io)
    p.dma(og_in.rearrange("(r u p) c -> r u p c", r=N_CORES, u=2),
          oh_int.rearrange("u p (r c) -> r u p c", c=T_CORE))
    p.collective("AllGather", [og_in], [og_all])
    ogv = og_all.rearrange("(src tb u p) c -> src tb u p c",
                           src=N_CORES, tb=N_CORES, u=2)
    o_ownv = o_own.rearrange("(b d) t -> b d t", b=B)

    def o_fn(role):
        return [(o_ownv.rearrange("b d t -> (b d) t").rearrange(
                    "(s u p) c -> s u p c", s=N_CORES, u=2),
                 ogv[:, role])]
    p.switch_dmas(o_fn, 1)

    io = {"oin": o_ownv,
          "xin": own(bufA), "woT": w["woT5"], "bo": w["bo5"],
          "xout": own(bufB)}
    emit_b5c(p, nc, env, io)
    io = {"xin": own(bufB), "mats": mats, "bia": bia, "inc": inc,
          "xout": xout_f}
    emit_b4(p, nc, env, io)
    p.finish()
    return nc


def kernel_fused(**inp):
    x_fm = np.ascontiguousarray(
        np.asarray(inp["L0_attn_x"]).transpose(0, 2, 1))
    xs = _tok_slices(x_fm, HALO1)
    nc = _prog("fused", build_fused)
    zstrip = np.zeros((B * D, HALO1), np.float32)
    cmask = _causal_masks()
    maps = []
    for c in range(N_CORES):
        m = {
            "x0": xs[c],
            "zstrip": zstrip,
            "ident": _IDENT,
            "tabs": _rotary_tables(c * T_CORE - (HALO1 - 1), W1 - 1),
            "lmask": _local_masks(c),
            "bmask": _band_masks(c),
            "cmask": cmask,
        }
        for l in range(2):
            m[f"wqkv{l}"] = np.asarray(inp["L0_ffn_wqkv"])[l]
            m[f"wo{l}"] = np.asarray(inp["L0_ffn_wo"])[l]
            m[f"qs{l}"] = np.tile(np.asarray(inp["L0_ffn_q_scale"])[l],
                                  2).astype(np.float32)
            m[f"ks{l}"] = np.tile(np.asarray(inp["L0_ffn_k_scale"])[l],
                                  2).astype(np.float32)
            m[f"w1{l}"] = np.asarray(inp["L0_ffn_w1"])[l]
            b1v = np.asarray(inp["L0_ffn_b1"])[l]
            b1a = np.zeros(NJ * 128, np.float32)
            b1a[:HID] = b1v[:HID]
            b1g = np.zeros(NJ * 128, np.float32)
            b1g[:HID] = b1v[HID:]
            m[f"b1a{l}"] = b1a
            m[f"b1g{l}"] = b1g
            m[f"w2{l}"] = np.asarray(inp["L0_ffn_w2"])[l]
            m[f"b2{l}"] = np.asarray(inp["L0_ffn_b2"])[l]
            m[f"lng{l}"] = np.asarray(inp["L0_ffn_ln_g"])[l]
            m[f"lnb{l}"] = np.asarray(inp["L0_ffn_ln_b"])[l]
        m["wqkv3"] = np.asarray(inp["L0_attn_W_qkv"])
        m["bqkv3"] = np.asarray(inp["L0_attn_b_qkv"])
        m["lnw3"] = np.asarray(inp["L0_attn_ln_w"])
        m["lnb3"] = np.asarray(inp["L0_attn_ln_b"])
        for nm in ("wf", "uf", "wi", "ui", "wc", "uc", "wo", "uo"):
            m[nm + "T"] = np.ascontiguousarray(
                np.asarray(inp["L1_ffn_" + nm]).T)
        for nm in ("bf", "bi", "bc", "bo"):
            m[nm] = np.asarray(inp["L1_ffn_" + nm])
        m["inc"] = np.asarray(inp["L1_ffn_in_c"])
        m["wqT"] = np.ascontiguousarray(np.asarray(inp["L1_attn_Wq"]).T)
        m["wkT"] = np.ascontiguousarray(np.asarray(inp["L1_attn_Wk"]).T)
        m["wvT"] = np.ascontiguousarray(np.asarray(inp["L1_attn_Wv"]).T)
        m["woT5"] = np.ascontiguousarray(np.asarray(inp["L1_attn_Wo"]).T)
        m["bq5"] = np.asarray(inp["L1_attn_bq"])
        m["bk5"] = np.asarray(inp["L1_attn_bk"])
        m["bv5"] = np.asarray(inp["L1_attn_bv"])
        m["bo5"] = np.asarray(inp["L1_attn_bo"])
        maps.append(m)
    res = _run(nc, maps)
    out = np.concatenate([r["xout"] for r in res.results], axis=2)
    return np.ascontiguousarray(out.transpose(0, 2, 1))


# version 07
# speedup vs baseline: 1.0486x; 1.0486x over previous
import math
import contextlib
import numpy as np
import ml_dtypes
import concourse.bass as bass
import concourse.mybir as mybir
from concourse.bass_utils import run_bass_kernel_spmd

F32 = mybir.dt.float32
BF16 = mybir.dt.bfloat16
AF = mybir.ActivationFunctionType
ALU = mybir.AluOpType
AX = mybir.AxisListType

N_CORES = 8
D = 1024
T_FULL = 16384
B = 2
T_CORE = T_FULL // N_CORES

BF16_NP = ml_dtypes.bfloat16


class Prog:
    ENGINES = ("tensor", "vector", "scalar", "gpsimd", "sync")

    def __init__(self, nc):
        self.nc = nc
        self.items = []
        self.cnt = 0
        self.stack = contextlib.ExitStack()
        self._uid = 0

    def _nm(self, name):
        self._uid += 1
        return f"{name}__{self._uid}"

    def sb(self, name, shape, dtype):
        return self.stack.enter_context(
            self.nc.sbuf_tensor(self._nm(name), shape, dtype))

    def cvec(self, val, dtype=F32):
        key = (float(val), dtype)
        if not hasattr(self, "_consts"):
            self._consts = {}
        if key not in self._consts:
            t = self.sb(f"cv{len(self._consts)}", [128, 1], dtype)
            self.memset(t[:], float(val), engine="gpsimd")
            self._consts[key] = t
        return self._consts[key][:, 0:1]

    def ps(self, name, shape, dtype=F32):
        return self.stack.enter_context(
            self.nc.psum_tensor(self._nm(name), shape, dtype))

    def scope(self):
        return contextlib.ExitStack()

    def sbs(self, scope, name, shape, dtype):
        return scope.enter_context(
            self.nc.sbuf_tensor(self._nm(name), shape, dtype))

    def pss(self, scope, name, shape, dtype=F32):
        return scope.enter_context(
            self.nc.psum_tensor(self._nm(name), shape, dtype))

    def switch_dmas(self, fn, ndmas, engine="gpsimd"):
        self.items.append((engine, ("switch", fn, ndmas), self.cnt,
                           16 * ndmas))
        self.cnt += 16 * ndmas

    def collective(self, kind, ins, outs, op=ALU.bypass):
        def closure(e):
            return e.collective_compute(
                kind, op, replica_groups=[list(range(N_CORES))],
                ins=ins, outs=outs)
        self.emit("gpsimd", closure, 16)

    def emit(self, engine, closure, inc=1):
        self.items.append((engine, closure, self.cnt, inc))
        self.cnt += inc

    def dma(self, out, in_, engine="sync", **kw):
        self.emit(engine, lambda e: e.dma_start(out=out, in_=in_, **kw), 16)

    def mm(self, out, lhsT, rhs, start, stop, **kw):
        self.emit("tensor",
                  lambda e: e.matmul(out, lhsT, rhs, start=start, stop=stop,
                                     **kw), 1)

    def transpose(self, out, in_, ident, **kw):
        self.emit("tensor", lambda e: e.transpose(out, in_, ident, **kw), 1)

    def act(self, out, in_, func, bias=0.0, scale=1.0, **kw):
        self.emit("scalar",
                  lambda e: e.activation(out, in_, func, bias=bias,
                                         scale=scale, **kw), 1)

    def tt(self, out, in0, in1, op, engine="vector"):
        self.emit(engine, lambda e: e.tensor_tensor(out, in0, in1, op), 1)

    def ts(self, out, in0, s1, s2, op0, op1=ALU.bypass, engine="vector"):
        self.emit(engine,
                  lambda e: e.tensor_scalar(out, in0, s1, s2, op0, op1), 1)

    def stt(self, out, in0, scalar, in1, op0, op1, engine="vector"):
        self.emit(engine,
                  lambda e: e.scalar_tensor_tensor(out, in0, scalar, in1,
                                                   op0, op1), 1)

    def copy(self, out, in_, engine="vector"):
        self.emit(engine, lambda e: e.tensor_copy(out, in_), 1)

    def memset(self, ap, const, engine="vector"):
        self.emit(engine, lambda e: e.memset(ap, const), 1)

    def recip(self, out, in_):
        self.emit("vector", lambda e: e.reciprocal(out, in_), 1)

    def addcol(self, out, in_, col):
        self.emit("scalar", lambda e: e.add(out, in_, col), 1)

    def finish(self):
        nc = self.nc
        with nc.semaphore("gchain") as G, nc.Block() as block:
            per_engine = {e: [] for e in self.ENGINES}
            for it in self.items:
                per_engine[it[0]].append(it)

            run_last = {}
            if True:
                idx_of = {}
                gi = 0
                for it in self.items:
                    idx_of[id(it)] = gi
                    gi += 1
                i = 0
                n = len(self.items)
                COAL = ("tensor",)
                while i < n:
                    eng, closure, before, inc = self.items[i]
                    if eng in COAL and not isinstance(closure, tuple):
                        j = i
                        tot = 0
                        while j < n and self.items[j][0] == eng and \
                                not isinstance(self.items[j][1], tuple):
                            tot += self.items[j][3]
                            j += 1
                        for k in range(i, j):
                            run_last[id(self.items[k])] = \
                                (k == j - 1, tot if k == j - 1 else 0,
                                 k == i)
                        i = j
                    else:
                        i += 1

            def make_section(eng_name):
                def section(eng):
                    for item in per_engine[eng_name]:
                        closure, before, inc = item[1], item[2], item[3]
                        rl = run_last.get(id(item))
                        if rl is not None:
                            is_last, tot, is_first = rl
                            if is_first and before > 0:
                                eng.wait_ge(G, before)
                            inst = closure(eng)
                            if is_last:
                                inst.then_inc(G, tot)
                            continue
                        if before > 0:
                            eng.wait_ge(G, before)
                        if isinstance(closure, tuple) and \
                                closure[0] == "switch":
                            _, fn, ndmas = closure
                            for arm in nc.switch_core_id([eng], n=128):
                                role = arm.logical % N_CORES
                                pairs = fn(role)
                                assert len(pairs) == ndmas
                                for o_ap, i_ap in pairs:
                                    eng.dma_start(out=o_ap, in_=i_ap
                                                  ).then_inc(G, 16)
                            continue
                        inst = closure(eng)
                        inst.then_inc(G, inc)
                    eng.wait_ge(G, self.cnt)
                return section

            for name in self.ENGINES:
                if per_engine[name]:
                    getattr(block, name)(make_section(name))
        self.stack.close()
        return nc


def decl_in(nc, name, shape):
    return nc.dram_tensor(name, list(shape), F32, kind="ExternalInput").ap()


def decl_in_bf(nc, name, shape):
    return nc.dram_tensor(name, list(shape), BF16, kind="ExternalInput").ap()


def decl_out(nc, name, shape):
    return nc.dram_tensor(name, list(shape), F32, kind="ExternalOutput").ap()


def load_bias_col(p, hbm_ap, n, name, sc=None):
    t = p.sb(name, [128, n // 128], F32) if sc is None else \
        p.sbs(sc, name, [128, n // 128], F32)
    p.dma(t[:], hbm_ap.rearrange("(c p) -> p c", p=128),
          allow_slow_non_contiguous=True)
    return t


def load_w_bf(p, w_ap, K, M, name, stage_cols=1536, sc=None):
    kb = (K + 127) // 128
    if sc is None:
        tiles = [p.sb(f"{name}_k{k}", [128, M], BF16) for k in range(kb)]
    else:
        tiles = [p.sbs(sc, f"{name}_k{k}", [128, M], BF16)
                 for k in range(kb)]
    sc = p.scope()
    stage = p.sbs(sc, f"{name}_stage", [128, min(M, stage_cols)], F32)
    for k in range(kb):
        ksz = min(128, K - k * 128)
        c0 = 0
        while c0 < M:
            cw = min(M - c0, stage.shape[1])
            p.dma(stage[0:ksz, 0:cw], w_ap[k * 128:k * 128 + ksz,
                                           c0:c0 + cw])
            p.copy(tiles[k][0:ksz, c0:c0 + cw], stage[0:ksz, 0:cw],
                   engine="gpsimd")
            c0 += cw
    sc.close()
    return tiles


class LNScratch:
    def __init__(self, p, W, tag=""):
        self.mean = p.sb(f"ln_mean{tag}", [128, W], F32)
        self.ex2 = p.sb(f"ln_ex2{tag}", [128, W], F32)
        self.rstd = p.sb(f"ln_rstd{tag}", [128, W], F32)
        self.tmp = p.sb(f"ln_tmp{tag}", [128, W], F32)
        self.xb = [p.sb(f"ln_xb{i}{tag}", [128, W], BF16) for i in range(8)]
        self.sqt = p.sb(f"ln_sq{tag}", [128, 512], BF16)
        self.ps_s = p.ps(f"ln_pss{tag}", [128, 512], F32)
        self.ps_q = p.ps(f"ln_psq{tag}", [128, 512], F32)


def emit_ln(p, x_tiles, W, sc, ones_bf, out_bf=None, out_f32=None,
            affine=None, x_bf16=False):
    if x_bf16:
        xb = x_tiles
    else:
        xb = sc.xb
        for blk in range(8):
            p.copy(xb[blk][:, 0:W], x_tiles[blk][:, 0:W], engine="gpsimd")
    nch = (W + 511) // 512
    for c in range(nch):
        c0 = c * 512
        cw = min(512, W - c0)
        for blk in range(8):
            p.mm(sc.ps_s[:, 0:cw], ones_bf[:, :], xb[blk][:, c0:c0 + cw],
                 start=(blk == 0), stop=(blk == 7))
        for blk in range(8):
            p.act(sc.sqt[:, 0:cw], xb[blk][:, c0:c0 + cw], AF.Square)
            p.mm(sc.ps_q[:, 0:cw], ones_bf[:, :], sc.sqt[:, 0:cw],
                 start=(blk == 0), stop=(blk == 7))
        p.act(sc.mean[:, c0:c0 + cw], sc.ps_s[:, 0:cw], AF.Copy,
              scale=1.0 / 1024.0)
        p.act(sc.ex2[:, c0:c0 + cw], sc.ps_q[:, 0:cw], AF.Copy,
              scale=1.0 / 1024.0)
    p.tt(sc.tmp[:, 0:W], sc.mean[:, 0:W], sc.mean[:, 0:W], ALU.mult)
    p.tt(sc.ex2[:, 0:W], sc.ex2[:, 0:W], sc.tmp[:, 0:W], ALU.subtract)
    p.act(sc.rstd[:, 0:W], sc.ex2[:, 0:W], AF.Sqrt, bias=p.cvec(1e-5))
    p.recip(sc.rstd[:, 0:W], sc.rstd[:, 0:W])
    for blk in range(8):
        p.tt(sc.tmp[:, 0:W], x_tiles[blk][:, 0:W], sc.mean[:, 0:W],
             ALU.subtract)
        if affine is not None:
            p.tt(sc.tmp[:, 0:W], sc.tmp[:, 0:W], sc.rstd[:, 0:W], ALU.mult)
            g_col, b_col = affine
            if out_bf is not None:
                p.ts(out_bf[blk][:, 0:W], sc.tmp[:, 0:W], g_col(blk),
                     b_col(blk), ALU.mult, ALU.add)
            if out_f32 is not None:
                p.ts(out_f32[blk][:, 0:W], sc.tmp[:, 0:W], g_col(blk),
                     b_col(blk), ALU.mult, ALU.add)
        else:
            if out_f32 is not None:
                p.tt(out_f32[blk][:, 0:W], sc.tmp[:, 0:W], sc.rstd[:, 0:W],
                     ALU.mult)
                if out_bf is not None:
                    p.copy(out_bf[blk][:, 0:W], out_f32[blk][:, 0:W],
                           engine="gpsimd")
            elif out_bf is not None:
                p.tt(out_bf[blk][:, 0:W], sc.tmp[:, 0:W], sc.rstd[:, 0:W],
                     ALU.mult)


HALO1 = 513
W1 = HALO1 + T_CORE
WH = W1 - 1


def emit_b1(p, nc, env, io, pre_ln):
    xin = io["xin"]
    wqkv = io["wqkv"]
    wo = io["wo"]
    qs = io["qs"]
    ks = io["ks"]
    tabs = io["tabs"]
    lmask = io["lmask"]
    xout = io["xout"]
    if pre_ln:
        u0out = io["u0out"]
    id_bf = env.id_bf
    ones_bf = env.ones_bf
    bd64 = env.bd64
    ssc = p.scope()
    wqkv_bf = load_w_bf(p, wqkv, D, 1536, "wqkv", sc=ssc)
    wo_bf = load_w_bf(p, wo, 512, D, "wo", sc=ssc)
    qs_sb = p.sbs(ssc, "qs_sb", [128, 1], F32)
    ks_sb = p.sbs(ssc, "ks_sb", [128, 1], F32)
    p.dma(qs_sb[:], qs.rearrange("(c p) -> p c", p=128),
          allow_slow_non_contiguous=True)
    p.dma(ks_sb[:], ks.rearrange("(c p) -> p c", p=128),
          allow_slow_non_contiguous=True)
    mask_sb = [p.sbs(ssc, f"msk{i}", [128, 512], BF16) for i in range(8)]
    o_all = [p.sbs(ssc, f"oall{i}", [128, T_CORE], BF16) for i in range(4)]
    tab_cs = p.sbs(ssc, "tab_cs", [128, 512], F32)
    tab_ss = p.sbs(ssc, "tab_ss", [128, 512], F32)
    rn_sb = p.sbs(ssc, "rn_sb", [128, 512], F32)
    sq_bf = p.sbs(ssc, "sq_bf", [128, 512], BF16)
    sw_bf = p.sbs(ssc, "sw_bf", [128, 512], BF16)
    vstore = p.sbs(ssc, "vstore", [128, 130], BF16)
    p.memset(vstore[:, 64:65], 1.0, engine="gpsimd")
    p.memset(vstore[:, 129:130], 1.0, engine="gpsimd")
    p_bf0 = p.sbs(ssc, "p_bf0", [128, 512], BF16)
    p_bf1 = p.sbs(ssc, "p_bf1", [128, 512], BF16)
    d_sb = p.sbs(ssc, "d_sb", [1, 512], F32)
    db_sb = p.sbs(ssc, "db_sb", [64, 512], F32)
    ones1_f = env.ones1_f
    res_sb = p.sbs(ssc, "res_sb", [128, 512], F32)
    out_sb = p.sbs(ssc, "out_sb", [128, 512], F32)

    ps_a = p.pss(ssc, "ps_a", [128, 512], F32)
    ps_b = p.pss(ssc, "ps_b", [128, 512], F32)
    ps_tr = p.pss(ssc, "ps_tr", [128, 128], BF16)
    ps_o0 = p.pss(ssc, "ps_o0", [65, 512], F32)
    ps_o1 = p.pss(ssc, "ps_o1", [65, 512], F32)
    ps_bc = p.pss(ssc, "ps_bc", [64, 512], F32)

    for b in range(B):
        bat_sc = p.scope()
        xc = [p.sbs(bat_sc, f"xc{i}_{b}", [128, W1], BF16) for i in range(8)]
        x_sc = p.scope()
        x8 = [p.sbs(x_sc, f"x{i}_{b}", [128, W1], F32) for i in range(8)]
        ln_sc = p.scope()
        lns = LNScratch2(p, ln_sc, ps_a, ps_b)
        for blk in range(8):
            p.dma(x8[blk][:], xin[b, blk * 128:(blk + 1) * 128, :])
        if pre_ln:
            emit_ln2(p, x8, W1, lns, ones_bf, out_f32=x8)
            for blk in range(8):
                p.dma(u0out[b, blk * 128:(blk + 1) * 128, :],
                      x8[blk][:, HALO1:W1])
        for blk in range(8):
            p.copy(xc[blk][:, :], x8[blk][:, :], engine="gpsimd")
        ln_sc.close()
        x_sc.close()

        def hsl(blk, c0, cw):
            if blk < 4:
                return xc[blk][:, c0 + 1:c0 + 1 + cw]
            return xc[blk][:, c0:c0 + cw]

        qkv_sc = p.scope()
        qkv_bf = [p.sbs(qkv_sc, f"qkv{i}_{b}", [128, WH], BF16)
                  for i in range(12)]
        for m in range(12):
            for c in range(5):
                c0 = c * 512
                for k in range(8):
                    p.mm(ps_a[:, :], wqkv_bf[k][:, m * 128:(m + 1) * 128],
                         hsl(k, c0, 512), start=(k == 0), stop=(k == 7))
                p.copy(qkv_bf[m][:, c0:c0 + 512], ps_a[:, :])

        for part, scol, tq in ((0, qs_sb, 0), (4, ks_sb, 2)):
            for c in range(5):
                c0 = c * 512
                p.dma(tab_cs[:], tabs[tq, :, c0:c0 + 512])
                p.dma(tab_ss[:], tabs[tq + 1, :, c0:c0 + 512])
                for j in range(4):
                    t = qkv_bf[part + j]
                    p.act(sq_bf[:, :], t[:, c0:c0 + 512], AF.Square)
                    p.mm(ps_a[:, :], bd64[:, :], sq_bf[:, :],
                         start=True, stop=True)
                    p.act(rn_sb[:, :], ps_a[:, :], AF.Sqrt)
                    p.ts(rn_sb[:, :], rn_sb[:, :], p.cvec(1e-12), None,
                         ALU.max)
                    p.recip(rn_sb[:, :], rn_sb[:, :])
                    p.stt(t[:, c0:c0 + 512], t[:, c0:c0 + 512],
                          scol[:, 0:1], rn_sb[:, :], ALU.mult, ALU.mult)
                    for half in range(4):
                        pr, psrc = half * 32, (half ^ 1) * 32
                        p.copy(sw_bf[pr:pr + 32, :],
                               t[psrc:psrc + 32, c0:c0 + 512])
                    p.tt(t[:, c0:c0 + 512], t[:, c0:c0 + 512],
                         tab_cs[:, :], ALU.mult)
                    p.tt(sw_bf[:, :], sw_bf[:, :], tab_ss[:, :], ALU.mult)
                    p.tt(t[:, c0:c0 + 512], t[:, c0:c0 + 512], sw_bf[:, :],
                         ALU.add)

        for w in range(4):
            for i in range(8):
                p.dma(mask_sb[i][:], lmask[w, i, :, :])
            q0 = 512 + 512 * w
            for hp in range(4):
                qt = qkv_bf[hp]
                kt = qkv_bf[4 + hp]
                vt = qkv_bf[8 + hp]
                for kti in range(8):
                    k0 = 512 * w + kti * 128
                    p.transpose(ps_tr[:, :], vt[:, k0:k0 + 128], id_bf[:])
                    p.copy(vstore[:, 0:64], ps_tr[:, 0:64])
                    p.copy(vstore[:, 65:129], ps_tr[:, 64:128])
                    p.mm(ps_a[:, :], kt[0:64, k0:k0 + 128],
                         qt[0:64, q0:q0 + 512], start=True, stop=True,
                         tile_position=(0, 0))
                    p.mm(ps_b[:, :], kt[64:128, k0:k0 + 128],
                         qt[64:128, q0:q0 + 512], start=True, stop=True,
                         tile_position=(64, 0))
                    p.act(p_bf0[:, :], ps_a[:, :], AF.Exp,
                          bias=p.cvec(-20.0), scale=8.0)
                    p.act(p_bf1[:, :], ps_b[:, :], AF.Exp,
                          bias=p.cvec(-20.0), scale=8.0)
                    p.tt(p_bf0[:, :], p_bf0[:, :], mask_sb[kti][:, :],
                         ALU.mult)
                    p.tt(p_bf1[:, :], p_bf1[:, :], mask_sb[kti][:, :],
                         ALU.mult)
                    p.mm(ps_o0[:, :], vstore[:, 0:65], p_bf0[:, :],
                         start=(kti == 0), stop=(kti == 7))
                    p.mm(ps_o1[:, :], vstore[:, 65:130], p_bf1[:, :],
                         start=(kti == 0), stop=(kti == 7))
                oc0 = q0 - 512
                for hi, ps_o in ((0, ps_o0), (1, ps_o1)):
                    p.recip(d_sb[:, :], ps_o[64:65, :])
                    p.mm(ps_bc[:, :], ones1_f[:, :], d_sb[:, :],
                         start=True, stop=True)
                    p.copy(db_sb[:, :], ps_bc[:, :])
                    p.tt(o_all[hp][hi * 64:(hi + 1) * 64, oc0:oc0 + 512],
                         ps_o[0:64, :], db_sb[:, :], ALU.mult)
        qkv_sc.close()
        bat_sc.close()

        usrc = u0out if pre_ln else xin
        uoff = 0 if pre_ln else HALO1
        for m in range(8):
            for c in range(4):
                c0 = c * 512
                for k in range(4):
                    p.mm(ps_a[:, :], wo_bf[k][:, m * 128:(m + 1) * 128],
                         o_all[k][:, c0:c0 + 512],
                         start=(k == 0), stop=(k == 3))
                p.dma(res_sb[:, :],
                      usrc[b, m * 128:(m + 1) * 128,
                           uoff + c0:uoff + c0 + 512])
                p.tt(out_sb[:, :], res_sb[:, :], ps_a[:, :], ALU.add)
                p.dma(xout[b, m * 128:(m + 1) * 128, c0:c0 + 512],
                      out_sb[:, :])
    ssc.close()


def build_b1(pre_ln):
    nc = bass.Bass("TRN2", target_bir_lowering=False)
    io = {
        "xin": decl_in(nc, "xin", [B, D, W1]),
        "wqkv": decl_in(nc, "wqkv", [D, 1536]),
        "wo": decl_in(nc, "wo", [512, D]),
        "qs": decl_in(nc, "qs", [128]),
        "ks": decl_in(nc, "ks", [128]),
        "tabs": decl_in(nc, "tabs", [4, 128, WH]),
        "lmask": decl_in_bf(nc, "lmask", [4, 8, 128, 512]),
        "xout": decl_out(nc, "xout", [B, D, T_CORE]),
    }
    ident = decl_in(nc, "ident", [128, 128])
    if pre_ln:
        io["u0out"] = decl_out(nc, "u0out", [B, D, T_CORE])
    p = Prog(nc)
    env = Env(p, ident)
    emit_b1(p, nc, env, io, pre_ln)
    p.finish()
    return nc


class Env:
    def __init__(self, p, ident_ap):
        self.id_f = p.sb("id_f", [128, 128], F32)
        self.id_bf = p.sb("id_bf", [128, 128], BF16)
        p.dma(self.id_f[:], ident_ap)
        p.copy(self.id_bf[:], self.id_f[:], engine="gpsimd")
        self.ones_bf = p.sb("ones_bf", [128, 128], BF16)
        p.memset(self.ones_bf[:], 1.0, engine="gpsimd")
        self.bd64 = p.sb("bd64", [128, 128], BF16)
        p.memset(self.bd64[:], 0.0, engine="gpsimd")
        p.memset(self.bd64[0:64, 0:64], 1.0, engine="gpsimd")
        p.memset(self.bd64[64:128, 64:128], 1.0, engine="gpsimd")
        self.ones1_f = p.sb("ones1_f", [1, 64], F32)
        p.memset(self.ones1_f[:], 1.0, engine="gpsimd")
        for v in (1e-5, 1e-12, -20.0, 0.5):
            p.cvec(v)


_LN_CNT = [0]


class LNScratch2:
    def __init__(self, p, scope, ps_s, ps_q):
        n = _LN_CNT[0]; _LN_CNT[0] += 1
        self.mean = p.sbs(scope, f"ln_mean{n}", [128, 512], F32)
        self.rstd = p.sbs(scope, f"ln_rstd{n}", [128, 512], F32)
        self.tmp = p.sbs(scope, f"ln_tmp{n}", [128, 512], F32)
        self.xb = p.sbs(scope, f"ln_xb{n}", [128, 512], BF16)
        self.ps_s = ps_s
        self.ps_q = ps_q


def emit_ln2(p, x_tiles, W, sc, ones_bf, out_bf=None, out_f32=None,
             affine=None):
    if callable(x_tiles):
        xs = x_tiles
    else:
        xs = lambda blk, c0, cw: x_tiles[blk][:, c0:c0 + cw]
    nch = (W + 511) // 512
    for c in range(nch):
        c0 = c * 512
        cw = min(512, W - c0)
        for blk in range(8):
            p.copy(sc.xb[:, 0:cw], xs(blk, c0, cw), engine="gpsimd")
            p.mm(sc.ps_s[:, 0:cw], ones_bf[:, :], sc.xb[:, 0:cw],
                 start=(blk == 0), stop=(blk == 7))
            p.act(sc.xb[:, 0:cw], xs(blk, c0, cw), AF.Square)
            p.mm(sc.ps_q[:, 0:cw], ones_bf[:, :], sc.xb[:, 0:cw],
                 start=(blk == 0), stop=(blk == 7))
        p.act(sc.mean[:, 0:cw], sc.ps_s[:, 0:cw], AF.Copy,
              scale=1.0 / 1024.0)
        p.act(sc.rstd[:, 0:cw], sc.ps_q[:, 0:cw], AF.Copy,
              scale=1.0 / 1024.0)
        p.tt(sc.tmp[:, 0:cw], sc.mean[:, 0:cw], sc.mean[:, 0:cw], ALU.mult)
        p.tt(sc.rstd[:, 0:cw], sc.rstd[:, 0:cw], sc.tmp[:, 0:cw],
             ALU.subtract)
        p.act(sc.rstd[:, 0:cw], sc.rstd[:, 0:cw], AF.Sqrt,
              bias=p.cvec(1e-5))
        p.recip(sc.rstd[:, 0:cw], sc.rstd[:, 0:cw])
        for blk in range(8):
            p.tt(sc.tmp[:, 0:cw], xs(blk, c0, cw), sc.mean[:, 0:cw],
                 ALU.subtract)
            if affine is not None:
                g_col, b_col = affine
                p.tt(sc.tmp[:, 0:cw], sc.tmp[:, 0:cw], sc.rstd[:, 0:cw],
                     ALU.mult)
                if out_bf is not None:
                    p.ts(out_bf[blk][:, c0:c0 + cw], sc.tmp[:, 0:cw],
                         g_col(blk), b_col(blk), ALU.mult, ALU.add)
                if out_f32 is not None:
                    p.ts(out_f32[blk][:, c0:c0 + cw], sc.tmp[:, 0:cw],
                         g_col(blk), b_col(blk), ALU.mult, ALU.add)
            else:
                if out_f32 is not None:
                    p.tt(out_f32[blk][:, c0:c0 + cw], sc.tmp[:, 0:cw],
                         sc.rstd[:, 0:cw], ALU.mult)
                    if out_bf is not None:
                        p.copy(out_bf[blk][:, c0:c0 + cw],
                               out_f32[blk][:, c0:c0 + cw], engine="gpsimd")
                elif out_bf is not None:
                    p.tt(out_bf[blk][:, c0:c0 + cw], sc.tmp[:, 0:cw],
                         sc.rstd[:, 0:cw], ALU.mult)


DIM_HEAD = 64
WINDOW = 512


def _rotary_tables(t0, ncols):
    half = np.arange(0, DIM_HEAD, 2, dtype=np.float64)
    inv_freq = 1.0 / 10000.0 ** (half / DIM_HEAD)
    t = (np.arange(ncols, dtype=np.float64) + t0)
    freqs = t[:, None] * inv_freq[None, :]
    base = (half + 0.4 * DIM_HEAD) / (1.4 * DIM_HEAD)
    power = (t - T_FULL // 2) / float(WINDOW // 2)
    scale = base[None, :] ** power[:, None]
    cos, sin = np.cos(freqs), np.sin(freqs)
    cs_q = (cos * scale).T
    ss_q = (sin * scale).T
    cs_k = (cos / scale).T
    ss_k = (sin / scale).T

    def pack(cs, ss):
        CS = np.concatenate([cs, cs], axis=0)
        SS = np.concatenate([-ss, ss], axis=0)
        CS = np.concatenate([CS, CS], axis=0)
        SS = np.concatenate([SS, SS], axis=0)
        return CS, SS
    CSq, SSq = pack(cs_q, ss_q)
    CSk, SSk = pack(cs_k, ss_k)
    return np.stack([CSq, SSq, CSk, SSk]).astype(np.float32)


def _local_masks(core):
    out = np.zeros((4, 8, 128, 512), dtype=np.float32)
    p = np.arange(128)[:, None]
    c = np.arange(512)[None, :]
    for w in range(4):
        qglob0 = core * T_CORE + w * 512
        for kti in range(8):
            rel = kti * 128 - 512
            d = rel + p - c
            keep = (d <= 0) & (d >= -WINDOW)
            jglob = qglob0 + rel + p + 0 * c
            keep = keep & (jglob >= 0)
            out[w, kti] = keep
    return out.astype(BF16_NP)


def _band_masks(core):
    out = np.zeros((4, 6, 128, 512), dtype=np.float32)
    p = np.arange(128)[:, None]
    c = np.arange(512)[None, :]
    for qt in range(4):
        qglob0 = core * T_CORE + qt * 512
        for kti in range(6):
            rel = kti * 128 - 256
            d = rel + p - c
            keep = (d <= 0) & (d >= -255)
            jglob = qglob0 + rel + p + 0 * c
            keep = keep & (jglob >= 0)
            out[qt, kti] = keep / 512.0
    return out.astype(BF16_NP)


def _causal_masks():
    out = np.zeros((4, 128, 512), dtype=np.float32)
    p = np.arange(128)[:, None]
    c = np.arange(512)[None, :]
    for i in range(4):
        rel = i * 128
        out[i] = (rel + p) <= c
    return out.astype(BF16_NP)


def _tok_slices(x_fm, halo):
    xp = np.pad(x_fm, ((0, 0), (0, 0), (halo, 0)))
    return [np.ascontiguousarray(xp[:, :, c * T_CORE:c * T_CORE + halo
                                    + T_CORE]) for c in range(N_CORES)]


_IDENT = np.eye(128, dtype=np.float32)


HID = 2730
NJ = 22


def _jsz(j):
    return 128 if j < 21 else HID - 21 * 128


def emit_b2(p, nc, env, io, combine):
    xin = io["xin"]
    w1 = io["w1"]
    b1a = io["b1a"]
    b1g = io["b1g"]
    w2 = io["w2"]
    b2 = io["b2"]
    lng = io["lng"]
    lnb = io["lnb"]
    if combine:
        xorig = io["xorig"]
    xout = io["xout"]
    ones_bf = env.ones_bf
    ssc = p.scope()
    w1_bf = load_w_bf(p, w1, D, 2 * HID, "w1", sc=ssc)
    w2_bf = load_w_bf(p, w2, HID, D, "w2", sc=ssc)
    b1ac = load_bias_col(p, b1a, NJ * 128, "b1ac", sc=ssc)
    b1gc = load_bias_col(p, b1g, NJ * 128, "b1gc", sc=ssc)
    b2c = load_bias_col(p, b2, D, "b2c", sc=ssc)
    lngc = load_bias_col(p, lng, D, "lngc", sc=ssc)
    lnbc = load_bias_col(p, lnb, D, "lnbc", sc=ssc)
    hn_bf = [p.sbs(ssc, f"hn{i}", [128, 512], BF16) for i in range(8)]
    glu_bf = [p.sbs(ssc, f"glu{i}", [128, 512], BF16) for i in range(NJ)]
    gg_bf = p.sbs(ssc, "gg_bf", [128, 512], BF16)
    a_sb = p.sbs(ssc, "a_sb", [128, 512], F32)
    t_sb = p.sbs(ssc, "t_sb", [128, 512], F32)
    r_sb = p.sbs(ssc, "r_sb", [128, 512], F32)
    out_sb = p.sbs(ssc, "out_sb", [128, 512], F32)
    ps_a = p.pss(ssc, "ps_a", [128, 512], F32)
    ps_b = p.pss(ssc, "ps_b", [128, 512], F32)
    ps_s = p.pss(ssc, "ps_s", [128, 512], F32)
    ps_q = p.pss(ssc, "ps_q", [128, 512], F32)

    for b in range(B):
        for c in range(4):
            c0 = c * 512
            ch_sc = p.scope()
            x9 = [p.sbs(ch_sc, f"x{i}_{b}_{c}", [128, 513], F32)
                  for i in range(8)]
            for blk in range(8):
                p.dma(x9[blk][:], xin[b, blk * 128:(blk + 1) * 128,
                                      c0:c0 + 513])

            def hsl(blk, cc0, cw):
                off = 1 if blk < 4 else 0
                return x9[blk][:, off + cc0:off + cc0 + cw]

            ln_sc = p.scope()
            lns = LNScratch2(p, ln_sc, ps_s, ps_q)
            emit_ln2(p, hsl, 512, lns, ones_bf, out_bf=hn_bf,
                     affine=(lambda blk: lngc[:, blk:blk + 1],
                             lambda blk: lnbc[:, blk:blk + 1]))
            ln_sc.close()

            for j in range(NJ):
                sz = _jsz(j)
                for k in range(8):
                    p.mm(ps_a[0:sz, :], w1_bf[k][:, j * 128:j * 128 + sz],
                         hn_bf[k][:, :], start=(k == 0), stop=(k == 7))
                for k in range(8):
                    p.mm(ps_b[0:sz, :],
                         w1_bf[k][:, HID + j * 128:HID + j * 128 + sz],
                         hn_bf[k][:, :], start=(k == 0), stop=(k == 7))
                p.act(gg_bf[0:sz, :], ps_b[0:sz, :], AF.Gelu,
                      bias=b1gc[0:sz, j:j + 1])
                p.emit("scalar",
                       lambda e, s=sz, jj=j: e.add(
                           a_sb[0:s, :], ps_a[0:s, :],
                           b1ac[0:s, jj:jj + 1]), 1)
                p.tt(glu_bf[j][0:sz, :], a_sb[0:sz, :], gg_bf[0:sz, :],
                     ALU.mult)

            for m in range(8):
                for j in range(NJ):
                    sz = _jsz(j)
                    p.mm(ps_a[:, :], w2_bf[j][0:sz, m * 128:(m + 1) * 128],
                         glu_bf[j][0:sz, :], start=(j == 0),
                         stop=(j == NJ - 1))
                p.emit("scalar",
                       lambda e, mm=m: e.add(t_sb[:, :], ps_a[:, :],
                                             b2c[:, mm:mm + 1]), 1)
                xsl = x9[m][:, 1:513]
                if combine:
                    p.dma(r_sb[:, :], xorig[b, m * 128:(m + 1) * 128,
                                            c0:c0 + 512])
                    p.stt(out_sb[:, :], xsl, p.cvec(0.5), r_sb[:, :],
                          ALU.mult, ALU.add)
                    p.stt(out_sb[:, :], t_sb[:, :], p.cvec(0.5),
                          out_sb[:, :], ALU.mult, ALU.add)
                else:
                    p.tt(out_sb[:, :], xsl, t_sb[:, :], ALU.add)
                p.dma(xout[b, m * 128:(m + 1) * 128, c0:c0 + 512],
                      out_sb[:, :])
            ch_sc.close()
    ssc.close()


def build_b2(combine):
    nc = bass.Bass("TRN2", target_bir_lowering=False)
    io = {
        "xin": decl_in(nc, "xin", [B, D, T_CORE + 1]),
        "w1": decl_in(nc, "w1", [D, 2 * HID]),
        "b1a": decl_in(nc, "b1a", [NJ * 128]),
        "b1g": decl_in(nc, "b1g", [NJ * 128]),
        "w2": decl_in(nc, "w2", [HID, D]),
        "b2": decl_in(nc, "b2", [D]),
        "lng": decl_in(nc, "lng", [D]),
        "lnb": decl_in(nc, "lnb", [D]),
        "xout": decl_out(nc, "xout", [B, D, T_CORE]),
    }
    ident = decl_in(nc, "ident", [128, 128])
    if combine:
        io["xorig"] = decl_in(nc, "xorig", [B, D, T_CORE])
    p = Prog(nc)
    env = Env(p, ident)
    emit_b2(p, nc, env, io, combine)
    p.finish()
    return nc


HALO3 = 256
W3 = HALO3 + T_CORE


def emit_b3(p, nc, env, io):
    xin = io["xin"]
    wqkv = io["wqkv"]
    bqkv = io["bqkv"]
    lnw = io["lnw"]
    lnb = io["lnb"]
    bmask = io["bmask"]
    xout = io["xout"]
    id_bf = env.id_bf
    ones_bf = env.ones_bf
    ssc = p.scope()
    bqc = load_bias_col(p, bqkv, 3072, "bqc", sc=ssc)
    lnwc = load_bias_col(p, lnw, D, "lnwc", sc=ssc)
    lnbc = load_bias_col(p, lnb, D, "lnbc", sc=ssc)
    r_sb = p.sbs(ssc, "r_sb", [128, 512], F32)
    out_sb = p.sbs(ssc, "out_sb", [128, 512], F32)
    ps_a = p.pss(ssc, "ps_a", [128, 512], F32)
    ps_b = p.pss(ssc, "ps_b", [128, 512], F32)
    ps_tr = p.pss(ssc, "ps_tr", [128, 128], BF16)
    ps_o = [p.pss(ssc, f"ps_o{i}", [128, 512], F32) for i in range(4)]

    for b in range(B):
        qkv_sc = p.scope()
        q_bf = [p.sbs(qkv_sc, f"q{i}_{b}", [128, T_CORE], BF16)
                for i in range(8)]
        k_bf = [p.sbs(qkv_sc, f"k{i}_{b}", [128, W3], BF16)
                for i in range(8)]
        v_bf = [p.sbs(qkv_sc, f"v{i}_{b}", [128, W3], BF16)
                for i in range(8)]
        xn_sc = p.scope()
        xn = [p.sbs(xn_sc, f"xn{i}_{b}", [128, W3], BF16) for i in range(8)]
        x_sc = p.scope()
        x8c = [p.sbs(x_sc, f"xc{i}_{b}", [128, 512], F32) for i in range(8)]
        ln_sc = p.scope()
        lns = LNScratch2(p, ln_sc, ps_a, ps_b)
        nch = (W3 + 511) // 512
        for c in range(nch):
            c0 = c * 512
            cw = min(512, W3 - c0)
            for blk in range(8):
                p.dma(x8c[blk][:, 0:cw],
                      xin[b, blk * 128:(blk + 1) * 128, c0:c0 + cw])
            emit_ln2(p, x8c, cw, lns, ones_bf,
                     out_bf=[t[:, c0:c0 + cw] for t in xn])
        ln_sc.close()
        x_sc.close()

        wqk_sc = p.scope()
        wqk = [p.sbs(wqk_sc, f"wqk{k}_{b}", [128, 2048], BF16)
               for k in range(8)]
        st_sc = p.scope()
        stg = p.sbs(st_sc, f"stg_{b}", [128, 2048], F32)
        for k in range(8):
            p.dma(stg[:], wqkv[k * 128:(k + 1) * 128, 0:2048])
            p.copy(wqk[k][:], stg[:], engine="gpsimd")
        st_sc.close()
        for m in range(16):
            cols = range(4) if m < 8 else range(5)
            for c in cols:
                if m < 8:
                    c0s = HALO3 + c * 512
                    cw = 512
                else:
                    c0s = c * 512
                    cw = min(512, W3 - c0s)
                for k in range(8):
                    p.mm(ps_a[:, 0:cw], wqk[k][:, m * 128:(m + 1) * 128],
                         xn[k][:, c0s:c0s + cw],
                         start=(k == 0), stop=(k == 7))
                dst = (q_bf[m][:, c * 512:c * 512 + cw] if m < 8
                       else k_bf[m - 8][:, c0s:c0s + cw])
                p.addcol(dst, ps_a[:, 0:cw], bqc[:, m:m + 1])
        wqk_sc.close()
        wv_sc = p.scope()
        wv = [p.sbs(wv_sc, f"wv{k}_{b}", [128, 1024], BF16)
              for k in range(8)]
        st_sc = p.scope()
        stg = p.sbs(st_sc, f"stv_{b}", [128, 1024], F32)
        for k in range(8):
            p.dma(stg[:], wqkv[k * 128:(k + 1) * 128, 2048:3072])
            p.copy(wv[k][:], stg[:], engine="gpsimd")
        st_sc.close()
        for m in range(8):
            for c in range(5):
                c0s = c * 512
                cw = min(512, W3 - c0s)
                for k in range(8):
                    p.mm(ps_a[:, 0:cw], wv[k][:, m * 128:(m + 1) * 128],
                         xn[k][:, c0s:c0s + cw],
                         start=(k == 0), stop=(k == 7))
                p.addcol(v_bf[m][:, c0s:c0s + cw], ps_a[:, 0:cw],
                         bqc[:, 16 + m:17 + m])
        wv_sc.close()
        xn_sc.close()

        att_sc = p.scope()
        mask_sb = [p.sbs(att_sc, f"msk{i}_{b}", [128, 512], BF16)
                   for i in range(6)]
        vt_sb = [p.sbs(att_sc, f"vt{i}_{b}", [128, D], BF16)
                 for i in range(6)]
        s_bf = [p.sbs(att_sc, f"sbf{i}_{b}", [128, 512], BF16)
                for i in range(6)]
        o_f = [p.sbs(att_sc, f"of{i}_{b}", [128, 512], F32)
               for i in range(8)]
        for qt in range(4):
            for i in range(6):
                p.dma(mask_sb[i][:], bmask[qt, i, :, :])
            for kti in range(6):
                ktc = qt * 512 + kti * 128
                for k in range(8):
                    p.mm(ps_a[:, :], k_bf[k][:, ktc:ktc + 128],
                         q_bf[k][:, qt * 512:qt * 512 + 512],
                         start=(k == 0), stop=(k == 7))
                p.tt(s_bf[kti][:, :], ps_a[:, :], mask_sb[kti][:, :],
                     ALU.mult)
                for vb in range(8):
                    p.transpose(ps_tr[:, :], v_bf[vb][:, ktc:ktc + 128],
                                id_bf[:])
                    p.copy(vt_sb[kti][:, vb * 128:(vb + 1) * 128],
                           ps_tr[:, :])
            for half in range(2):
                for mi in range(4):
                    mb = half * 4 + mi
                    for kti in range(6):
                        p.mm(ps_o[mi][:, :],
                             vt_sb[kti][:, mb * 128:(mb + 1) * 128],
                             s_bf[kti][:, :], start=(kti == 0),
                             stop=(kti == 5))
                    p.copy(o_f[mb][:, :], ps_o[mi][:, :])
            ln_sc = p.scope()
            lns = LNScratch2(p, ln_sc, ps_a, ps_b)
            emit_ln2(p, o_f, 512, lns, ones_bf, out_f32=o_f,
                     affine=(lambda blk: lnwc[:, blk:blk + 1],
                             lambda blk: lnbc[:, blk:blk + 1]))
            ln_sc.close()
            for m in range(8):
                p.dma(r_sb[:, :], xin[b, m * 128:(m + 1) * 128,
                                      HALO3 + qt * 512:HALO3 + qt * 512
                                      + 512])
                p.tt(out_sb[:, :], r_sb[:, :], o_f[m][:, :], ALU.add)
                p.dma(xout[b, m * 128:(m + 1) * 128,
                           qt * 512:qt * 512 + 512], out_sb[:, :])
        att_sc.close()
        qkv_sc.close()
    ssc.close()


def build_b3():
    nc = bass.Bass("TRN2", target_bir_lowering=False)
    io = {
        "xin": decl_in(nc, "xin", [B, D, W3]),
        "wqkv": decl_in(nc, "wqkv", [D, 3072]),
        "bqkv": decl_in(nc, "bqkv", [3072]),
        "lnw": decl_in(nc, "lnw", [D]),
        "lnb": decl_in(nc, "lnb", [D]),
        "bmask": decl_in_bf(nc, "bmask", [4, 6, 128, 512]),
        "xout": decl_out(nc, "xout", [B, D, T_CORE]),
    }
    ident = decl_in(nc, "ident", [128, 128])
    p = Prog(nc)
    env = Env(p, ident)
    emit_b3(p, nc, env, io)
    p.finish()
    return nc


def emit_b4(p, nc, env, io):
    xin = io["xin"]
    mats = io["mats"]
    bia = io["bia"]
    inc = io["inc"]
    xout = io["xout"]
    ones_bf = env.ones_bf
    ssc = p.scope()
    bcol = {nm: load_bias_col(p, bia[nm], D, nm + "c", sc=ssc)
            for nm in bia}
    incc = load_bias_col(p, inc, D, "incc", sc=ssc)
    xn = [[p.sbs(ssc, f"xn{t}_{i}", [128, T_CORE], BF16) for i in range(8)]
          for t in range(2)]
    g_f = p.sbs(ssc, "g_f", [128, 512], F32)
    g_i = p.sbs(ssc, "g_i", [128, 512], F32)
    g_c = p.sbs(ssc, "g_c", [128, 512], F32)
    t_sb = p.sbs(ssc, "t_sb", [128, 512], F32)
    t2_sb = p.sbs(ssc, "t2_sb", [128, 512], F32)
    r_sb = p.sbs(ssc, "r_sb", [128, 512], F32)
    ps_a = p.pss(ssc, "ps_a", [128, 512], F32)
    ps_b = p.pss(ssc, "ps_b", [128, 512], F32)
    ps_c = p.pss(ssc, "ps_c", [128, 512], F32)
    ps_s = p.pss(ssc, "ps_s", [128, 512], F32)
    ps_q = p.pss(ssc, "ps_q", [128, 512], F32)

    for t in range(2):
        x_sc = p.scope()
        x8c = [p.sbs(x_sc, f"x{t}_{i}", [128, 512], F32) for i in range(8)]
        ln_sc = p.scope()
        lns = LNScratch2(p, ln_sc, ps_s, ps_q)
        for c in range(4):
            c0 = c * 512
            for blk in range(8):
                p.dma(x8c[blk][:], xin[t, blk * 128:(blk + 1) * 128,
                                       c0:c0 + 512])
            emit_ln2(p, x8c, 512, lns, ones_bf,
                     out_bf=[z[:, c0:c0 + 512] for z in xn[t]])
        ln_sc.close()
        x_sc.close()

    c_f = [p.sbs(ssc, f"cf{i}", [128, T_CORE], F32) for i in range(8)]
    cb = [p.sbs(ssc, f"cb{i}", [128, T_CORE], BF16) for i in range(8)]
    for m in range(8):
        p.memset(c_f[m][:], 0.0, engine="gpsimd")
        p.addcol(c_f[m][:], c_f[m][:], incc[:, m:m + 1])
        p.copy(cb[m][:], c_f[m][:], engine="gpsimd")

    _gw_cnt = [0]

    def gate_w(names, m, sc):
        tiles = {}
        n = _gw_cnt[0]
        _gw_cnt[0] += 1
        st = p.sbs(sc, f"gst{n}", [128, 128], F32)
        for nm in names:
            tl = []
            for k in range(8):
                tt_ = p.sbs(sc, f"g{nm}{n}_{k}", [128, 128], BF16)
                p.dma(st[:], mats[nm][k * 128:(k + 1) * 128,
                                      m * 128:(m + 1) * 128])
                p.copy(tt_[:], st[:], engine="gpsimd")
                tl.append(tt_)
            tiles[nm] = tl
        return tiles

    for step in range(2):
        xs = xn[step]
        for m in range(8):
            w_sc = p.scope()
            tw = gate_w(("wf", "uf", "wi", "ui", "wc", "uc"), m, w_sc)
            for c in range(4):
                c0 = c * 512
                for gname, wn, un, psx, gout, fn in (
                        ("f", "wf", "uf", ps_a, g_f, AF.Sigmoid),
                        ("i", "wi", "ui", ps_b, g_i, AF.Sigmoid),
                        ("c", "wc", "uc", ps_c, g_c, AF.Tanh)):
                    for k in range(8):
                        p.mm(psx[:, :], tw[wn][k][:], xs[k][:, c0:c0 + 512],
                             start=(k == 0), stop=False)
                    for k in range(8):
                        p.mm(psx[:, :], tw[un][k][:],
                             cb[k][:, c0:c0 + 512],
                             start=False, stop=(k == 7))
                    p.act(gout[:, :], psx[:, :], fn,
                          bias=bcol["b" + gname][:, m:m + 1])
                p.tt(t_sb[:, :], g_f[:, :], c_f[m][:, c0:c0 + 512],
                     ALU.mult)
                p.tt(t2_sb[:, :], g_i[:, :], g_c[:, :], ALU.mult)
                p.tt(c_f[m][:, c0:c0 + 512], t_sb[:, :], t2_sb[:, :],
                     ALU.add)
            w_sc.close()
        for m in range(8):
            p.copy(cb[m][:], c_f[m][:], engine="gpsimd")

    for t in range(2):
        for m in range(8):
            w_sc = p.scope()
            tw = gate_w(("wo", "uo"), m, w_sc)
            for c in range(4):
                c0 = c * 512
                for k in range(8):
                    p.mm(ps_a[:, :], tw["wo"][k][:],
                         xn[t][k][:, c0:c0 + 512],
                         start=(k == 0), stop=False)
                for k in range(8):
                    p.mm(ps_a[:, :], tw["uo"][k][:],
                         cb[k][:, c0:c0 + 512],
                         start=False, stop=(k == 7))
                p.act(t_sb[:, :], ps_a[:, :], AF.Tanh,
                      bias=bcol["bo"][:, m:m + 1])
                p.dma(r_sb[:, :], xin[t, m * 128:(m + 1) * 128,
                                      c0:c0 + 512])
                p.stt(t2_sb[:, :], t_sb[:, :], p.cvec(0.5), r_sb[:, :],
                      ALU.mult, ALU.add)
                p.dma(xout[t, m * 128:(m + 1) * 128, c0:c0 + 512],
                      t2_sb[:, :])
            w_sc.close()
    ssc.close()


def build_b4():
    nc = bass.Bass("TRN2", target_bir_lowering=False)
    mats = {}
    for nm in ("wf", "uf", "wi", "ui", "wc", "uc", "wo", "uo"):
        mats[nm] = decl_in(nc, nm + "T", [D, D])
    bia = {}
    for nm in ("bf", "bi", "bc", "bo"):
        bia[nm] = decl_in(nc, nm, [D])
    io = {
        "xin": decl_in(nc, "xin", [B, D, T_CORE]),
        "mats": mats, "bia": bia,
        "inc": decl_in(nc, "inc", [D]),
        "xout": decl_out(nc, "xout", [B, D, T_CORE]),
    }
    ident = decl_in(nc, "ident", [128, 128])
    p = Prog(nc)
    env = Env(p, ident)
    emit_b4(p, nc, env, io)
    p.finish()
    return nc


def emit_b5a(p, nc, env, io):
    xin = io["xin"]
    wqT, wkT, wvT = io["wqT"], io["wkT"], io["wvT"]
    bq, bk, bv = io["bq"], io["bk"], io["bv"]
    qf, kf, vt = io["qf"], io["kf"], io["vt"]
    id_bf = env.id_bf
    ones_bf = env.ones_bf
    ssc = p.scope()
    bqc = load_bias_col(p, bq, D, "bqc", sc=ssc)
    bkc = load_bias_col(p, bk, D, "bkc", sc=ssc)
    bvc = load_bias_col(p, bv, D, "bvc", sc=ssc)
    wq_bf = load_w_bf(p, wqT, D, D, "wq", sc=ssc)
    wk_bf = load_w_bf(p, wkT, D, D, "wk", sc=ssc)
    wv_bf = load_w_bf(p, wvT, D, D, "wv", sc=ssc)
    xn = [p.sbs(ssc, f"xn{i}", [128, T_CORE], BF16) for i in range(8)]
    o_bf = p.sbs(ssc, "o_bf", [128, 512], BF16)
    v_m = p.sbs(ssc, "v_m", [128, T_CORE], BF16)
    vt_s = p.sbs(ssc, "vt_s", [128, 128], BF16)
    ps_a = p.pss(ssc, "ps_a", [128, 512], F32)
    ps_s = p.pss(ssc, "ps_s", [128, 512], F32)
    ps_q = p.pss(ssc, "ps_q", [128, 512], F32)
    ps_tr = p.pss(ssc, "ps_tr", [128, 128], BF16)

    for b in range(B):
        x_sc = p.scope()
        x8c = [p.sbs(x_sc, f"x{b}_{i}", [128, 512], F32) for i in range(8)]
        ln_sc = p.scope()
        lns = LNScratch2(p, ln_sc, ps_s, ps_q)
        for c in range(4):
            c0 = c * 512
            for blk in range(8):
                p.dma(x8c[blk][:], xin[b, blk * 128:(blk + 1) * 128,
                                       c0:c0 + 512])
            emit_ln2(p, x8c, 512, lns, ones_bf,
                     out_bf=[z[:, c0:c0 + 512] for z in xn])
        ln_sc.close()
        x_sc.close()
        for w_bf, bc, dst in ((wq_bf, bqc, qf), (wk_bf, bkc, kf)):
            for m in range(8):
                for c in range(4):
                    c0 = c * 512
                    for k in range(8):
                        p.mm(ps_a[:, :], w_bf[k][:, m * 128:(m + 1) * 128],
                             xn[k][:, c0:c0 + 512],
                             start=(k == 0), stop=(k == 7))
                    p.addcol(o_bf[:, :], ps_a[:, :], bc[:, m:m + 1])
                    p.dma(dst[b, m * 128:(m + 1) * 128, c0:c0 + 512],
                          o_bf[:, :])
        for m in range(8):
            for c in range(4):
                c0 = c * 512
                for k in range(8):
                    p.mm(ps_a[:, :], wv_bf[k][:, m * 128:(m + 1) * 128],
                         xn[k][:, c0:c0 + 512],
                         start=(k == 0), stop=(k == 7))
                p.addcol(v_m[:, c0:c0 + 512], ps_a[:, :], bvc[:, m:m + 1])
            for tt in range(16):
                p.transpose(ps_tr[:, :], v_m[:, tt * 128:(tt + 1) * 128],
                            id_bf[:])
                p.copy(vt_s[:, :], ps_tr[:, :])
                p.dma(vt[b, tt * 128:(tt + 1) * 128,
                         m * 128:(m + 1) * 128], vt_s[:, :])
    ssc.close()


def build_b5a():
    nc = bass.Bass("TRN2", target_bir_lowering=False)
    io = {
        "xin": decl_in(nc, "xin", [B, D, T_CORE]),
        "wqT": decl_in(nc, "wqT", [D, D]),
        "wkT": decl_in(nc, "wkT", [D, D]),
        "wvT": decl_in(nc, "wvT", [D, D]),
        "bq": decl_in(nc, "bq", [D]),
        "bk": decl_in(nc, "bk", [D]),
        "bv": decl_in(nc, "bv", [D]),
        "qf": nc.dram_tensor("qf", [B, D, T_CORE], BF16,
                             kind="ExternalOutput").ap(),
        "kf": nc.dram_tensor("kf", [B, D, T_CORE], BF16,
                             kind="ExternalOutput").ap(),
        "vt": nc.dram_tensor("vt", [B, T_CORE, D], BF16,
                             kind="ExternalOutput").ap(),
    }
    ident = decl_in(nc, "ident", [128, 128])
    p = Prog(nc)
    env = Env(p, ident)
    emit_b5a(p, nc, env, io)
    p.finish()
    return nc


NQT = T_FULL // 512


def emit_b5b(p, nc, env, io):
    qh = io["qh"]
    kh = io["kh"]
    vh = io["vh"]
    cmask = io["cmask"]
    oh = io["oh"]
    ones1_f = env.ones1_f
    ssc = p.scope()
    mask_sb = [p.sbs(ssc, f"msk{i}", [128, 512], BF16) for i in range(4)]
    for i in range(4):
        p.dma(mask_sb[i][:], cmask[i, :, :])
    k_res = p.sbs(ssc, "k_res", [128, T_FULL], BF16)
    vt_res = p.sbs(ssc, "vt_res", [128, 130 * 128], BF16)
    p.memset(vt_res[:], 0.0, engine="gpsimd")
    q_sb = p.sbs(ssc, "q_sb", [128, 512], BF16)
    p_bf0 = p.sbs(ssc, "p_bf0", [128, 1024], BF16)
    p_bf1 = p.sbs(ssc, "p_bf1", [128, 1024], BF16)
    d_sb = p.sbs(ssc, "d_sb", [1, 512], F32)
    db_sb = p.sbs(ssc, "db_sb", [64, 512], F32)
    obuf = p.sbs(ssc, "obuf", [128, 512], BF16)
    ps_s0 = p.pss(ssc, "ps_s0", [128, 1024], F32)
    ps_s1 = p.pss(ssc, "ps_s1", [128, 1024], F32)
    ps_o0 = p.pss(ssc, "ps_o0", [65, 512], F32)
    ps_o1 = p.pss(ssc, "ps_o1", [65, 512], F32)
    ps_bc = p.pss(ssc, "ps_bc", [64, 512], F32)

    for ti in range(128):
        p.memset(vt_res[:, ti * 130 + 64:ti * 130 + 65], 1.0,
                 engine="gpsimd")
        p.memset(vt_res[:, ti * 130 + 129:ti * 130 + 130], 1.0,
                 engine="gpsimd")

    for u in range(2):
        for qt in range(NQT):
            t0 = qt * 512
            p.dma(k_res[:, t0:t0 + 512], kh[u, :, t0:t0 + 512])
            for ti4 in range(4):
                ti = qt * 4 + ti4
                tt0 = t0 + ti4 * 128
                p.dma(vt_res[:, ti * 130:ti * 130 + 64],
                      vh[u, tt0:tt0 + 128, 0:64])
                p.dma(vt_res[:, ti * 130 + 65:ti * 130 + 129],
                      vh[u, tt0:tt0 + 128, 64:128])
            p.dma(q_sb[:], qh[u, :, t0:t0 + 512])
            nkt = 4 * (qt + 1)
            for kt2 in range(nkt // 2):
                kts = (2 * kt2, 2 * kt2 + 1)
                for x2, kti in enumerate(kts):
                    k0 = kti * 128
                    c0 = x2 * 512
                    p.mm(ps_s0[:, c0:c0 + 512],
                         k_res[0:64, k0:k0 + 128], q_sb[0:64, :],
                         start=True, stop=True, tile_position=(0, 0))
                    p.mm(ps_s1[:, c0:c0 + 512],
                         k_res[64:128, k0:k0 + 128], q_sb[64:128, :],
                         start=True, stop=True, tile_position=(64, 0))
                p.act(p_bf0[:, :], ps_s0[:, :], AF.Exp,
                      bias=p.cvec(-20.0), scale=0.125)
                p.act(p_bf1[:, :], ps_s1[:, :], AF.Exp,
                      bias=p.cvec(-20.0), scale=0.125)
                for x2, kti in enumerate(kts):
                    if kti >= nkt - 4:
                        mi = kti - (nkt - 4)
                        c0 = x2 * 512
                        p.tt(p_bf0[:, c0:c0 + 512], p_bf0[:, c0:c0 + 512],
                             mask_sb[mi][:, :], ALU.mult)
                        p.tt(p_bf1[:, c0:c0 + 512], p_bf1[:, c0:c0 + 512],
                             mask_sb[mi][:, :], ALU.mult)
                for x2, kti in enumerate(kts):
                    c0 = x2 * 512
                    p.mm(ps_o0[:, :], vt_res[:, kti * 130:kti * 130 + 65],
                         p_bf0[:, c0:c0 + 512], start=(kti == 0),
                         stop=(kti == nkt - 1))
                    p.mm(ps_o1[:, :],
                         vt_res[:, kti * 130 + 65:kti * 130 + 130],
                         p_bf1[:, c0:c0 + 512], start=(kti == 0),
                         stop=(kti == nkt - 1))
            for hi, ps_o in ((0, ps_o0), (1, ps_o1)):
                p.recip(d_sb[:, :], ps_o[64:65, :])
                p.mm(ps_bc[:, :], ones1_f[:, :], d_sb[:, :],
                     start=True, stop=True)
                p.copy(db_sb[:, :], ps_bc[:, :])
                p.tt(obuf[hi * 64:(hi + 1) * 64, :], ps_o[0:64, :],
                     db_sb[:, :], ALU.mult)
            p.dma(oh[u, :, t0:t0 + 512], obuf[:, :])
    ssc.close()


def build_b5b():
    nc = bass.Bass("TRN2", target_bir_lowering=False)
    io = {
        "qh": decl_in_bf(nc, "qh", [2, 128, T_FULL]),
        "kh": decl_in_bf(nc, "kh", [2, 128, T_FULL]),
        "vh": decl_in_bf(nc, "vh", [2, T_FULL, 128]),
        "cmask": decl_in_bf(nc, "cmask", [4, 128, 512]),
        "oh": nc.dram_tensor("oh", [2, 128, T_FULL], BF16,
                             kind="ExternalOutput").ap(),
    }
    ident = decl_in(nc, "ident", [128, 128])
    p = Prog(nc)
    env = Env(p, ident)
    emit_b5b(p, nc, env, io)
    p.finish()
    return nc


def emit_b5c(p, nc, env, io):
    oin = io["oin"]
    xin = io["xin"]
    woT = io["woT"]
    bo = io["bo"]
    xout = io["xout"]
    ssc = p.scope()
    wo_bf = load_w_bf(p, woT, D, D, "wo", sc=ssc)
    boc = load_bias_col(p, bo, D, "boc", sc=ssc)
    o_bf = [p.sbs(ssc, f"o{i}", [128, T_CORE], BF16) for i in range(8)]
    r_sb = p.sbs(ssc, "r_sb", [128, 512], F32)
    t_sb = p.sbs(ssc, "t_sb", [128, 512], F32)
    out_sb = p.sbs(ssc, "out_sb", [128, 512], F32)
    ps_a = p.pss(ssc, "ps_a", [128, 512], F32)

    for b in range(B):
        for blk in range(8):
            p.dma(o_bf[blk][:], oin[b, blk * 128:(blk + 1) * 128, :])
        for m in range(8):
            for c in range(4):
                c0 = c * 512
                for k in range(8):
                    p.mm(ps_a[:, :], wo_bf[k][:, m * 128:(m + 1) * 128],
                         o_bf[k][:, c0:c0 + 512],
                         start=(k == 0), stop=(k == 7))
                p.addcol(t_sb[:, :], ps_a[:, :], boc[:, m:m + 1])
                p.dma(r_sb[:, :], xin[b, m * 128:(m + 1) * 128,
                                      c0:c0 + 512])
                p.tt(out_sb[:, :], r_sb[:, :], t_sb[:, :], ALU.add)
                p.dma(xout[b, m * 128:(m + 1) * 128, c0:c0 + 512],
                      out_sb[:, :])
    ssc.close()


def build_b5c():
    nc = bass.Bass("TRN2", target_bir_lowering=False)
    io = {
        "oin": decl_in_bf(nc, "oin", [B, D, T_CORE]),
        "xin": decl_in(nc, "xin", [B, D, T_CORE]),
        "woT": decl_in(nc, "woT", [D, D]),
        "bo": decl_in(nc, "bo", [D]),
        "xout": decl_out(nc, "xout", [B, D, T_CORE]),
    }
    ident = decl_in(nc, "ident", [128, 128])
    p = Prog(nc)
    env = Env(p, ident)
    emit_b5c(p, nc, env, io)
    p.finish()
    return nc


_PROGS = {}


def _prog(key, fn, *a):
    if key not in _PROGS:
        _PROGS[key] = fn(*a)
    return _PROGS[key]


def _run(nc, maps):
    return run_bass_kernel_spmd(nc, maps, core_ids=list(range(N_CORES)))


def _gather_fm(res, key):
    return np.concatenate([r[key] for r in res.results], axis=2)


def _l0_ffn_call(x_fm, w, tabs_pc, lmask_pc):
    x0 = x_fm
    u = None
    for l in range(2):
        pre = (l == 0)
        comb = (l == 1)
        src = x0 if pre else u
        xs = _tok_slices(src, HALO1)
        nc = _prog(("b1", pre), build_b1, pre)
        maps = []
        for c in range(N_CORES):
            maps.append({
                "xin": xs[c],
                "wqkv": w["wqkv"][l], "wo": w["wo"][l],
                "qs": np.tile(w["q_scale"][l], 2).astype(np.float32),
                "ks": np.tile(w["k_scale"][l], 2).astype(np.float32),
                "tabs": tabs_pc[c], "lmask": lmask_pc[c], "ident": _IDENT,
            })
        res = _run(nc, maps)
        u = _gather_fm(res, "xout")
        xs2 = _tok_slices(u, 1)
        nc2 = _prog(("b2", comb), build_b2, comb)
        b1v = w["b1"][l]
        b1a = np.zeros(NJ * 128, np.float32)
        b1a[:HID] = b1v[:HID]
        b1g = np.zeros(NJ * 128, np.float32)
        b1g[:HID] = b1v[HID:]
        maps2 = []
        for c in range(N_CORES):
            m = {"xin": xs2[c], "w1": w["w1"][l], "b1a": b1a, "b1g": b1g,
                 "w2": w["w2"][l], "b2": w["b2"][l], "lng": w["ln_g"][l],
                 "lnb": w["ln_b"][l], "ident": _IDENT}
            if comb:
                m["xorig"] = np.ascontiguousarray(
                    x0[:, :, c * T_CORE:(c + 1) * T_CORE])
            maps2.append(m)
        res2 = _run(nc2, maps2)
        u = _gather_fm(res2, "xout")
    return u


def kernel(**inp):
    x_fm = np.ascontiguousarray(
        np.asarray(inp["L0_attn_x"]).transpose(0, 2, 1))

    tabs_pc = [_rotary_tables(c * T_CORE - (HALO1 - 1), W1 - 1)
               for c in range(N_CORES)]
    lmask_pc = [_local_masks(c) for c in range(N_CORES)]
    bmask_pc = [_band_masks(c) for c in range(N_CORES)]
    cmask = _causal_masks()

    l0w = {k: np.asarray(inp["L0_ffn_" + k]) for k in
           ("wqkv", "q_scale", "k_scale", "wo", "ln_g", "ln_b", "w1", "b1",
            "w2", "b2")}

    x_fm = _l0_ffn_call(x_fm, l0w, tabs_pc, lmask_pc)
    xs = _tok_slices(x_fm, HALO3)
    nc = _prog("b3", build_b3)
    maps = [{"xin": xs[c], "wqkv": np.asarray(inp["L0_attn_W_qkv"]),
             "bqkv": np.asarray(inp["L0_attn_b_qkv"]),
             "lnw": np.asarray(inp["L0_attn_ln_w"]),
             "lnb": np.asarray(inp["L0_attn_ln_b"]),
             "bmask": bmask_pc[c], "ident": _IDENT}
            for c in range(N_CORES)]
    x_fm = _gather_fm(_run(nc, maps), "xout")
    x_fm = _l0_ffn_call(x_fm, l0w, tabs_pc, lmask_pc)

    def l1_ffn_call(x_fm):
        xs = _tok_slices(x_fm, 0)
        nc = _prog("b4", build_b4)
        maps = []
        for c in range(N_CORES):
            m = {"xin": xs[c], "inc": np.asarray(inp["L1_ffn_in_c"]),
                 "ident": _IDENT}
            for nm in ("wf", "uf", "wi", "ui", "wc", "uc", "wo", "uo"):
                m[nm + "T"] = np.ascontiguousarray(
                    np.asarray(inp["L1_ffn_" + nm]).T)
            for nm in ("bf", "bi", "bc", "bo"):
                m[nm] = np.asarray(inp["L1_ffn_" + nm])
            maps.append(m)
        return _gather_fm(_run(nc, maps), "xout")

    x_fm = l1_ffn_call(x_fm)

    xs = _tok_slices(x_fm, 0)
    nc = _prog("b5a", build_b5a)
    maps = [{"xin": xs[c],
             "wqT": np.ascontiguousarray(np.asarray(inp["L1_attn_Wq"]).T),
             "wkT": np.ascontiguousarray(np.asarray(inp["L1_attn_Wk"]).T),
             "wvT": np.ascontiguousarray(np.asarray(inp["L1_attn_Wv"]).T),
             "bq": np.asarray(inp["L1_attn_bq"]),
             "bk": np.asarray(inp["L1_attn_bk"]),
             "bv": np.asarray(inp["L1_attn_bv"]),
             "ident": _IDENT}
            for c in range(N_CORES)]
    res = _run(nc, maps)
    qf = np.concatenate([r["qf"] for r in res.results], axis=2)
    kf = np.concatenate([r["kf"] for r in res.results], axis=2)
    vt = np.concatenate([r["vt"] for r in res.results], axis=1)

    units = [(b, hp) for b in range(B) for hp in range(8)]
    nc = _prog("b5b", build_b5b)
    maps = []
    for c in range(N_CORES):
        us = units[2 * c:2 * c + 2]
        maps.append({
            "qh": np.stack([qf[b, hp * 128:(hp + 1) * 128, :]
                            for b, hp in us]),
            "kh": np.stack([kf[b, hp * 128:(hp + 1) * 128, :]
                            for b, hp in us]),
            "vh": np.stack([vt[b, :, hp * 128:(hp + 1) * 128]
                            for b, hp in us]),
            "cmask": cmask, "ident": _IDENT,
        })
    res = _run(nc, maps)
    o_fm = np.zeros((B, D, T_FULL), BF16_NP)
    for c in range(N_CORES):
        for i, (b, hp) in enumerate(units[2 * c:2 * c + 2]):
            o_fm[b, hp * 128:(hp + 1) * 128, :] =                 np.asarray(res.results[c]["oh"][i]).view(BF16_NP)                 if res.results[c]["oh"].dtype != BF16_NP                 else res.results[c]["oh"][i]

    nc = _prog("b5c", build_b5c)
    maps = [{"oin": np.ascontiguousarray(
                 o_fm[:, :, c * T_CORE:(c + 1) * T_CORE]),
             "xin": np.ascontiguousarray(
                 x_fm[:, :, c * T_CORE:(c + 1) * T_CORE]),
             "woT": np.ascontiguousarray(np.asarray(inp["L1_attn_Wo"]).T),
             "bo": np.asarray(inp["L1_attn_bo"]), "ident": _IDENT}
            for c in range(N_CORES)]
    x_fm = _gather_fm(_run(nc, maps), "xout")

    x_fm = l1_ffn_call(x_fm)

    return np.ascontiguousarray(x_fm.transpose(0, 2, 1))


def build_fused():
    nc = bass.Bass("TRN2", target_bir_lowering=False)
    x0 = decl_in(nc, "x0", [B, D, W1])
    zstrip = decl_in(nc, "zstrip", [B * D, HALO1])
    ident = decl_in(nc, "ident", [128, 128])
    tabs = decl_in(nc, "tabs", [4, 128, WH])
    lmask = decl_in_bf(nc, "lmask", [4, 8, 128, 512])
    bmask = decl_in_bf(nc, "bmask", [4, 6, 128, 512])
    cmask = decl_in_bf(nc, "cmask", [4, 128, 512])
    w = {}
    for l in range(2):
        w[f"wqkv{l}"] = decl_in(nc, f"wqkv{l}", [D, 1536])
        w[f"wo{l}"] = decl_in(nc, f"wo{l}", [512, D])
        w[f"qs{l}"] = decl_in(nc, f"qs{l}", [128])
        w[f"ks{l}"] = decl_in(nc, f"ks{l}", [128])
        w[f"w1{l}"] = decl_in(nc, f"w1{l}", [D, 2 * HID])
        w[f"b1a{l}"] = decl_in(nc, f"b1a{l}", [NJ * 128])
        w[f"b1g{l}"] = decl_in(nc, f"b1g{l}", [NJ * 128])
        w[f"w2{l}"] = decl_in(nc, f"w2{l}", [HID, D])
        w[f"b2{l}"] = decl_in(nc, f"b2{l}", [D])
        w[f"lng{l}"] = decl_in(nc, f"lng{l}", [D])
        w[f"lnb{l}"] = decl_in(nc, f"lnb{l}", [D])
    w["wqkv3"] = decl_in(nc, "wqkv3", [D, 3072])
    w["bqkv3"] = decl_in(nc, "bqkv3", [3072])
    w["lnw3"] = decl_in(nc, "lnw3", [D])
    w["lnb3"] = decl_in(nc, "lnb3", [D])
    mats = {nm: decl_in(nc, nm + "T", [D, D])
            for nm in ("wf", "uf", "wi", "ui", "wc", "uc", "wo", "uo")}
    bia = {nm: decl_in(nc, nm, [D]) for nm in ("bf", "bi", "bc", "bo")}
    inc = decl_in(nc, "inc", [D])
    for nm in ("wqT", "wkT", "wvT", "woT5"):
        w[nm] = decl_in(nc, nm, [D, D])
    for nm in ("bq5", "bk5", "bv5", "bo5"):
        w[nm] = decl_in(nc, nm, [D])
    xout_f = decl_out(nc, "xout", [B, D, T_CORE])

    bufA = nc.dram_tensor("bufA", [B, D, W1], F32).ap()
    bufB = nc.dram_tensor("bufB", [B, D, W1], F32).ap()
    bufC = nc.dram_tensor("bufC", [B, D, W1], F32).ap()
    bufD = nc.dram_tensor("bufD", [B, D, W1], F32).ap()
    ud = nc.dram_tensor("ud", [B, D, T_CORE], F32).ap()
    strip = nc.dram_tensor("strip", [B * D, HALO1], F32).ap()
    hg = nc.dram_tensor("hg", [N_CORES * B * D, HALO1], F32).ap()
    qg_in = nc.dram_tensor("qg_in", [B * D, T_CORE], BF16).ap()
    kg_in = nc.dram_tensor("kg_in", [B * D, T_CORE], BF16).ap()
    vg_in = nc.dram_tensor("vg_in", [B * T_CORE, D], BF16).ap()
    qg = nc.dram_tensor("qg", [N_CORES * B * D, T_CORE], BF16).ap()
    kg = nc.dram_tensor("kg", [N_CORES * B * D, T_CORE], BF16).ap()
    vg = nc.dram_tensor("vg", [N_CORES * B * T_CORE, D], BF16).ap()
    qunit = nc.dram_tensor("qunit", [2, 128, T_FULL], BF16).ap()
    kunit = nc.dram_tensor("kunit", [2, 128, T_FULL], BF16).ap()
    vunit = nc.dram_tensor("vunit", [2, T_FULL, 128], BF16).ap()
    oh_int = nc.dram_tensor("oh_int", [2, 128, T_FULL], BF16).ap()
    og_in = nc.dram_tensor("og_in", [N_CORES * 2 * 128, T_CORE], BF16).ap()
    og_all = nc.dram_tensor("og_all",
                            [N_CORES * N_CORES * 2 * 128, T_CORE],
                            BF16).ap()
    o_own = nc.dram_tensor("o_own", [B * D, T_CORE], BF16).ap()

    p = Prog(nc)
    env = Env(p, ident)

    def own(buf):
        return buf[:, :, HALO1:W1]

    def exchange(buf):
        p.dma(strip.rearrange("(b d) h -> b d h", d=D),
              buf[:, :, W1 - HALO1:W1])
        p.collective("AllGather", [strip], [hg])
        hgv = hg.rearrange("(r bd) h -> r bd h", r=N_CORES)
        bufv = buf.rearrange("b d h -> (b d) h")

        def fn(role):
            src = (hgv[role - 1] if role > 0 else zstrip)
            return [(bufv[:, 0:HALO1], src)]
        p.switch_dmas(fn, 1)

    def l0f_call(cin, cout, tmp1, tmp2):
        io = {"xin": cin, "wqkv": w["wqkv0"], "wo": w["wo0"],
              "qs": w["qs0"], "ks": w["ks0"], "tabs": tabs,
              "lmask": lmask, "xout": own(tmp1), "u0out": ud}
        emit_b1(p, nc, env, io, True)
        exchange(tmp1)
        io = {"xin": tmp1[:, :, HALO1 - 1:W1], "w1": w["w10"],
              "b1a": w["b1a0"], "b1g": w["b1g0"], "w2": w["w20"],
              "b2": w["b20"], "lng": w["lng0"], "lnb": w["lnb0"],
              "xout": own(tmp2)}
        emit_b2(p, nc, env, io, False)
        exchange(tmp2)
        io = {"xin": tmp2, "wqkv": w["wqkv1"], "wo": w["wo1"],
              "qs": w["qs1"], "ks": w["ks1"], "tabs": tabs,
              "lmask": lmask, "xout": own(tmp1)}
        emit_b1(p, nc, env, io, False)
        exchange(tmp1)
        io = {"xin": tmp1[:, :, HALO1 - 1:W1], "w1": w["w11"],
              "b1a": w["b1a1"], "b1g": w["b1g1"], "w2": w["w21"],
              "b2": w["b21"], "lng": w["lng1"], "lnb": w["lnb1"],
              "xorig": own(cin), "xout": own(cout)}
        emit_b2(p, nc, env, io, True)

    l0f_call(x0, bufC, bufA, bufB)
    exchange(bufC)
    io = {"xin": bufC[:, :, HALO1 - HALO3:W1], "wqkv": w["wqkv3"],
          "bqkv": w["bqkv3"], "lnw": w["lnw3"], "lnb": w["lnb3"],
          "bmask": bmask, "xout": own(bufA)}
    emit_b3(p, nc, env, io)
    exchange(bufA)
    l0f_call(bufA, bufC, bufB, bufD)

    io = {"xin": own(bufC), "mats": mats, "bia": bia, "inc": inc,
          "xout": own(bufA)}
    emit_b4(p, nc, env, io)
    io = {"xin": own(bufA),
          "wqT": w["wqT"], "wkT": w["wkT"], "wvT": w["wvT"],
          "bq": w["bq5"], "bk": w["bk5"], "bv": w["bv5"],
          "qf": qg_in.rearrange("(b d) t -> b d t", b=B),
          "kf": kg_in.rearrange("(b d) t -> b d t", b=B),
          "vt": vg_in.rearrange("(b t) d -> b t d", b=B)}
    emit_b5a(p, nc, env, io)
    p.collective("AllGather", [qg_in], [qg])
    p.collective("AllGather", [kg_in], [kg])
    p.collective("AllGather", [vg_in], [vg])

    qgv = qg.rearrange("(r b d) t -> r b d t", r=N_CORES, b=B)
    kgv = kg.rearrange("(r b d) t -> r b d t", r=N_CORES, b=B)
    vgv = vg.rearrange("(r b t) d -> r b t d", r=N_CORES, b=B)
    quv = qunit.rearrange("u p (r c) -> u r p c", c=T_CORE)
    kuv = kunit.rearrange("u p (r c) -> u r p c", c=T_CORE)
    vuv = vunit.rearrange("u (r c) p -> u r c p", c=T_CORE)

    def unit_fn(role):
        pairs = []
        for i in range(2):
            uidx = 2 * role + i
            bb, hp = uidx // 8, uidx % 8
            pairs.append((quv[i], qgv[:, bb, hp * 128:(hp + 1) * 128, :]))
            pairs.append((kuv[i], kgv[:, bb, hp * 128:(hp + 1) * 128, :]))
            for r in range(N_CORES):
                pairs.append((vuv[i, r],
                              vgv[r, bb, :, hp * 128:(hp + 1) * 128]))
        return pairs
    p.switch_dmas(unit_fn, 20)

    io = {"qh": qunit, "kh": kunit, "vh": vunit, "cmask": cmask,
          "oh": oh_int}
    emit_b5b(p, nc, env, io)
    p.dma(og_in.rearrange("(r u p) c -> r u p c", r=N_CORES, u=2),
          oh_int.rearrange("u p (r c) -> r u p c", c=T_CORE))
    p.collective("AllGather", [og_in], [og_all])
    ogv = og_all.rearrange("(src tb u p) c -> src tb u p c",
                           src=N_CORES, tb=N_CORES, u=2)
    o_ownv = o_own.rearrange("(b d) t -> b d t", b=B)

    def o_fn(role):
        return [(o_ownv.rearrange("b d t -> (b d) t").rearrange(
                    "(s u p) c -> s u p c", s=N_CORES, u=2),
                 ogv[:, role])]
    p.switch_dmas(o_fn, 1)

    io = {"oin": o_ownv,
          "xin": own(bufA), "woT": w["woT5"], "bo": w["bo5"],
          "xout": own(bufB)}
    emit_b5c(p, nc, env, io)
    io = {"xin": own(bufB), "mats": mats, "bia": bia, "inc": inc,
          "xout": xout_f}
    emit_b4(p, nc, env, io)
    p.finish()
    return nc


def kernel_fused(**inp):
    x_fm = np.ascontiguousarray(
        np.asarray(inp["L0_attn_x"]).transpose(0, 2, 1))
    xs = _tok_slices(x_fm, HALO1)
    nc = _prog("fused", build_fused)
    zstrip = np.zeros((B * D, HALO1), np.float32)
    cmask = _causal_masks()
    maps = []
    for c in range(N_CORES):
        m = {
            "x0": xs[c],
            "zstrip": zstrip,
            "ident": _IDENT,
            "tabs": _rotary_tables(c * T_CORE - (HALO1 - 1), W1 - 1),
            "lmask": _local_masks(c),
            "bmask": _band_masks(c),
            "cmask": cmask,
        }
        for l in range(2):
            m[f"wqkv{l}"] = np.asarray(inp["L0_ffn_wqkv"])[l]
            m[f"wo{l}"] = np.asarray(inp["L0_ffn_wo"])[l]
            m[f"qs{l}"] = np.tile(np.asarray(inp["L0_ffn_q_scale"])[l],
                                  2).astype(np.float32)
            m[f"ks{l}"] = np.tile(np.asarray(inp["L0_ffn_k_scale"])[l],
                                  2).astype(np.float32)
            m[f"w1{l}"] = np.asarray(inp["L0_ffn_w1"])[l]
            b1v = np.asarray(inp["L0_ffn_b1"])[l]
            b1a = np.zeros(NJ * 128, np.float32)
            b1a[:HID] = b1v[:HID]
            b1g = np.zeros(NJ * 128, np.float32)
            b1g[:HID] = b1v[HID:]
            m[f"b1a{l}"] = b1a
            m[f"b1g{l}"] = b1g
            m[f"w2{l}"] = np.asarray(inp["L0_ffn_w2"])[l]
            m[f"b2{l}"] = np.asarray(inp["L0_ffn_b2"])[l]
            m[f"lng{l}"] = np.asarray(inp["L0_ffn_ln_g"])[l]
            m[f"lnb{l}"] = np.asarray(inp["L0_ffn_ln_b"])[l]
        m["wqkv3"] = np.asarray(inp["L0_attn_W_qkv"])
        m["bqkv3"] = np.asarray(inp["L0_attn_b_qkv"])
        m["lnw3"] = np.asarray(inp["L0_attn_ln_w"])
        m["lnb3"] = np.asarray(inp["L0_attn_ln_b"])
        for nm in ("wf", "uf", "wi", "ui", "wc", "uc", "wo", "uo"):
            m[nm + "T"] = np.ascontiguousarray(
                np.asarray(inp["L1_ffn_" + nm]).T)
        for nm in ("bf", "bi", "bc", "bo"):
            m[nm] = np.asarray(inp["L1_ffn_" + nm])
        m["inc"] = np.asarray(inp["L1_ffn_in_c"])
        m["wqT"] = np.ascontiguousarray(np.asarray(inp["L1_attn_Wq"]).T)
        m["wkT"] = np.ascontiguousarray(np.asarray(inp["L1_attn_Wk"]).T)
        m["wvT"] = np.ascontiguousarray(np.asarray(inp["L1_attn_Wv"]).T)
        m["woT5"] = np.ascontiguousarray(np.asarray(inp["L1_attn_Wo"]).T)
        m["bq5"] = np.asarray(inp["L1_attn_bq"])
        m["bk5"] = np.asarray(inp["L1_attn_bk"])
        m["bv5"] = np.asarray(inp["L1_attn_bv"])
        m["bo5"] = np.asarray(inp["L1_attn_bo"])
        maps.append(m)
    res = _run(nc, maps)
    out = np.concatenate([r["xout"] for r in res.results], axis=2)
    return np.ascontiguousarray(out.transpose(0, 2, 1))
